# version 08 (final)
# baseline (speedup 1.0000x reference)
import math
from contextlib import ExitStack

import numpy as np

import concourse.bass as bass
import concourse.mybir as mybir
from concourse.bass_utils import run_bass_kernel_spmd

F32 = mybir.dt.float32
BF16 = mybir.dt.bfloat16
U32 = mybir.dt.uint32
U8 = mybir.dt.uint8
AF = mybir.ActivationFunctionType
ALU = mybir.AluOpType
AX = mybir.AxisListType

B, T, D = 8, 4096, 1024
NCORES = 8



class P:

    ENG = {"t": "tensor", "v": "vector", "a": "scalar", "p": "gpsimd",
           "qs": "sync", "qa": "scalar", "qp": "gpsimd"}

    def __init__(self, name="mk"):
        self.nc = bass.Bass("TRN2", target_bir_lowering=False)
        self.ctx = ExitStack()
        self.sems = {}
        self.counts = {}
        self.waited = {k: {} for k in self.ENG}
        self.start_after = []
        self._uid = 0

    def uid(self):
        self._uid += 1
        return self._uid

    def sem(self, key):
        if key not in self.sems:
            self.sems[key] = self.ctx.enter_context(self.nc.semaphore(key))
            self.counts[key] = 0
        return self.sems[key]

    def sbuf(self, name, shape, dt=F32):
        return self.ctx.enter_context(self.nc.sbuf_tensor(name, shape, dt))

    def scope(self):
        from contextlib import contextmanager

        @contextmanager
        def _s():
            old_ctx = self.ctx
            old_sc = getattr(self, "_sconsts", {})
            self.ctx = ExitStack()
            self._sconsts = {}
            try:
                yield
            finally:
                self.ctx.close()
                self.ctx = old_ctx
                self._sconsts = old_sc
        return _s()

    def psum(self, name, shape, dt=F32):
        esz = {F32: 4, BF16: 2, U32: 4}.get(dt, 4)
        free = 1
        for d in shape[1:]:
            free *= d
        pad_free = ((free * esz + 2047) // 2048) * 2048 // esz
        if pad_free != free:
            t = self.ctx.enter_context(
                self.nc.psum_tensor(name, [shape[0], pad_free], dt))
            return t
        return self.ctx.enter_context(self.nc.psum_tensor(name, shape, dt))

    def dram(self, name, shape, dt=F32, kind="Internal"):
        return self.nc.dram_tensor(name, shape, dt, kind=kind).ap()

    def const(self, arr, name=None):
        arr = np.ascontiguousarray(arr)
        return self.nc.inline_tensor(arr, name or f"c{self.uid()}").ap()

    def eng(self, key):
        return getattr(self.nc, self.ENG[key])

    def _wait(self, ekey, tokens):
        e = self.eng(ekey)
        w = self.waited[ekey]
        for tk in tokens:
            if tk is None:
                continue
            (sname, val) = tk
            if val > w.get(sname, 0):
                e.wait_ge(self.sems[sname], val)
                w[sname] = val

    def scalar_const(self, val, dt=F32):
        key = (float(val), dt)
        if not hasattr(self, "_sconsts"):
            self._sconsts = {}
        if key not in self._sconsts:
            t = self.sbuf(f"sc_{self.uid()}", [128, 1], dt)
            tok = self.op("v", "memset", t[:], float(val))
            self._sconsts[key] = (t, tok)
        return self._sconsts[key]

    def frontier(self):
        return [(k, v) for k, v in self.counts.items() if v > 0]

    def op(self, ekey, fname, *args, after=(), **kw):
        if self.start_after:
            after = list(after) + self.start_after
        self._wait(ekey, after)
        try:
            inst = getattr(self.eng(ekey), fname)(*args, **kw)
        except ValueError:
            import sys as _sys
            print("OP FAIL", ekey, fname,
                  [getattr(a, "shape", a) for a in args], kw,
                  file=_sys.stderr)
            raise
        step = 16 if fname in ("dma_start", "dma_start_transpose",
                               "indirect_dma_start", "collective_compute") else 1
        skey = "s_" + ekey
        self.sem(skey)
        self.counts[skey] += step
        inst.then_inc(self.sems[skey], step)
        return (skey, self.counts[skey])

    def raw(self, ekey, fname, *args, after=(), **kw):
        self._wait(ekey, after)
        return getattr(self.eng(ekey), fname)(*args, **kw)

    def dma(self, qkey, out, in_, after=(), **kw):
        return self.op(qkey, "dma_start", after=after, out=out, in_=in_, **kw)

    def finish(self, tokens):
        self._wait("qs", tokens)


def ts(i, n):
    return slice(i * n, (i + 1) * n)



def emit_layernorm_tile(p, x_tm, ln_tm, n_free, after=(), eps=1e-5, stats=None):
    s = stats
    nb = (n_free + 511) // 512
    tks = []
    for i in range(nb):
        w = min(512, n_free - i * 512)
        tks.append(p.op("v", "bn_stats", s["bnt"][:, i * 6:(i + 1) * 6],
                        x_tm[:, i * 512:i * 512 + w], after=after))
    t_ag = p.op("v", "bn_aggr", s["mv"][:], s["bnt"][:, 0:6 * nb], after=tks)
    eps_ap, eps_tok = p.scalar_const(eps)
    t_sq = p.op("a", "activation", s["std"][:], s["mv"][:, 1:2], AF.Sqrt,
                bias=eps_ap[:], after=[t_ag, eps_tok])
    t_r = p.op("v", "reciprocal", s["rstd"][:], s["std"][:], after=[t_sq])
    t_nm = p.op("v", "tensor_scalar_mul", s["negmean"][:], s["mv"][:, 0:1],
                -1.0, after=[t_ag])
    t_ln = p.op("v", "tensor_scalar", ln_tm, x_tm, s["negmean"][:],
                s["rstd"][:], ALU.add, ALU.mult, after=[t_r, t_nm] + list(after))
    return t_ln


def ln_stats_bufs(p, tag):
    return {
        "bnt": p.sbuf(f"bnt_{tag}", [128, 12], F32),
        "mv": p.sbuf(f"mv_{tag}", [128, 2], F32),
        "std": p.sbuf(f"std_{tag}", [128, 1], F32),
        "rstd": p.sbuf(f"rstd_{tag}", [128, 1], F32),
        "negmean": p.sbuf(f"negmean_{tag}", [128, 1], F32),
    }


def bcast16(ap, reps=16):
    return bass.AP(ap.tensor, ap.offset,
                   [list(ap.ap[0]), list(ap.ap[1]), [0, reps]])


def quant_bufs(p, tag, nf):
    nb = nf // 16
    return {
        "nf": nf,
        "bm": p.sbuf(f"q_bm_{tag}", [128, nb], F32),
        "sc": p.sbuf(f"q_sc_{tag}", [128, nb], F32),
        "rs": p.sbuf(f"q_rs_{tag}", [128, nb], F32),
        "xn": p.sbuf(f"q_xn_{tag}", [128, nf], F32),
        "ax": p.sbuf(f"q_ax_{tag}", [128, nf], F32),
        "bq": p.sbuf(f"q_bq_{tag}", [128, nf], BF16),
        "tq": p.sbuf(f"q_tq_{tag}", [128, nf], BF16),
        "sg": p.sbuf(f"q_sg_{tag}", [128, nf], BF16),
    }


def emit_quant(p, x_ap, out_ap, qb, nf=None, after=()):
    if nf is None:
        nf = qb["nf"]
    nb = nf // 16
    qb = {k: (v if k == "nf" else
              (v[:, :nb] if k in ("bm", "sc", "rs") else v[:, :nf]))
          for k, v in qb.items()}
    x3 = x_ap.rearrange("p (b s) -> p b s", s=16)
    t0 = p.op("v", "tensor_reduce", qb["bm"], x3, AX.X, ALU.max,
              apply_absolute_value=True, after=after)
    t1 = p.op("v", "tensor_scalar", qb["sc"], qb["bm"], 1.0 / 6.0, 1e-12,
              ALU.mult, ALU.max, after=[t0])
    t2 = p.op("v", "reciprocal", qb["rs"], qb["sc"], after=[t1])
    xn3 = qb["xn"].rearrange("p (b s) -> p b s", s=16)
    t3 = p.op("v", "tensor_tensor", xn3, x3, bcast16(qb["rs"]), ALU.mult,
              after=[t2] + list(after))
    t4 = p.op("a", "activation", qb["ax"], qb["xn"], AF.Abs, after=[t3])
    t5 = p.op("a", "activation", qb["sg"], qb["xn"], AF.Sign, after=[t3])
    b0 = p.op("v", "tensor_scalar", qb["bq"], qb["ax"], 5.0, 2.0,
              ALU.is_gt, ALU.mult, after=[t4])
    b1 = p.op("v", "scalar_tensor_tensor", qb["bq"], qb["ax"], 3.5,
              qb["bq"], ALU.is_gt, ALU.add, after=[b0])
    b2 = p.op("v", "scalar_tensor_tensor", qb["bq"], qb["ax"], 2.5,
              qb["bq"], ALU.is_gt, ALU.add, after=[b1])
    c0 = p.op("v", "tensor_scalar", qb["tq"], qb["ax"], 0.25, None,
              ALU.is_gt, after=[t4])
    c1 = p.op("v", "scalar_tensor_tensor", qb["tq"], qb["ax"], 0.75,
              qb["tq"], ALU.is_gt, ALU.add, after=[c0])
    c2 = p.op("v", "scalar_tensor_tensor", qb["tq"], qb["ax"], 1.25,
              qb["tq"], ALU.is_gt, ALU.add, after=[c1])
    c3 = p.op("v", "scalar_tensor_tensor", qb["tq"], qb["ax"], 1.75,
              qb["tq"], ALU.is_gt, ALU.add, after=[c2])
    q0 = p.op("v", "scalar_tensor_tensor", qb["ax"], qb["tq"], 0.5,
              qb["bq"], ALU.mult, ALU.add, after=[b2, c3])
    ax3 = qb["ax"].rearrange("p (b s) -> p b s", s=16)
    s0 = p.op("v", "tensor_tensor", ax3, ax3, bcast16(qb["sc"]), ALU.mult,
              after=[q0, t1])
    t9 = p.op("v", "tensor_tensor", out_ap, qb["ax"], qb["sg"], ALU.mult,
              after=[s0, t5])
    return t9


def emit_transpose_fm(p, src_tm, dst_fm_fn, ident, ps_tr, nchunks, after=(),
                      dt_evac=None, evac_engine="v"):
    toks = []
    mm_tok = None
    for c in range(nchunks):
        mm_tok = p.op("t", "transpose", ps_tr[:, ts(c, 128)],
                      src_tm[:, ts(c, 128)], ident[:], after=after)
    out_toks = []
    for c in range(nchunks):
        out_toks.append(p.op(evac_engine, "tensor_copy", dst_fm_fn(c),
                             ps_tr[:, ts(c, 128)], after=[mm_tok]))
    return out_toks



FFN_I = 2816


def emit_ffn_weightprep(p, wg_ap, wu_ap, wd_ap, qb, identb):
    wgqT = p.sbuf("wgqT", [128, 8 * FFN_I], BF16)
    wuqT = p.sbuf("wuqT", [128, 8 * FFN_I], BF16)
    wdqT = p.sbuf("wdqT", [128, 22 * D], BF16)
    from contextlib import ExitStack as _ES
    _sc = _ES()
    wtile = _sc.enter_context(p.nc.sbuf_tensor("ffn_wtile", [128, FFN_I], F32))
    wqt = _sc.enter_context(p.nc.sbuf_tensor("ffn_wqt", [128, FFN_I], BF16))
    ps_tr = p.psum("ffn_wtr", [128, 1024], BF16)
    toks = []
    prev = []
    for (w_ap, dstT, nrows, nf, csz) in ((wg_ap, wgqT, FFN_I, D, FFN_I),
                                         (wu_ap, wuqT, FFN_I, D, FFN_I),
                                         (wd_ap, wdqT, D, FFN_I, D)):
        nchunks = nf // 128
        for r in range(nrows // 128):
            tl = p.dma("qs", wtile[:, :nf], w_ap[ts(r, 128), :], after=prev)
            tq = emit_quant(p, wtile[:, :nf], wqt[:, :nf], qb, nf=nf,
                            after=[tl])
            prev = []
            for cg in range(0, nchunks, 8):
                ng = min(8, nchunks - cg)
                trt = None
                for c in range(cg, cg + ng):
                    trt = p.op("t", "transpose", ps_tr[:, ts(c - cg, 128)],
                               wqt[:, ts(c, 128)], identb[:],
                               after=[tq] + prev)
                ev = []
                for c in range(cg, cg + ng):
                    ev.append(p.op("v", "tensor_copy",
                                   dstT[:, c * csz + r * 128:
                                        c * csz + r * 128 + 128],
                                   ps_tr[:, ts(c - cg, 128)], after=[trt]))
                prev = ev
        toks += prev
    _sc.close()
    return wgqT, wuqT, wdqT, toks


def emit_ffn_fp4(p, x_in, x_out, wgqT, wuqT, wdqT, qb, lnb, identb, tag,
                 w_tokens):
    x_tm = p.sbuf(f"ffn_x_{tag}", [128, D], F32)
    ln_tm = p.sbuf(f"ffn_ln_{tag}", [128, D], F32)
    xq_tm = p.sbuf(f"ffn_xq_{tag}", [128, D], BF16)
    xq_fm = p.sbuf(f"ffn_xqf_{tag}", [128, D], BF16)
    h_tm = p.sbuf(f"ffn_h_{tag}", [128, FFN_I], F32)
    hq_tm = p.sbuf(f"ffn_hq_{tag}", [128, FFN_I], BF16)
    hq_fm = p.sbuf(f"ffn_hqf_{tag}", [128, 8 * 128], BF16)
    sil = p.sbuf(f"ffn_sil_{tag}", [128, 512], F32)
    ps_tr = p.psum(f"ffn_tr_{tag}", [128, 1024], BF16)
    ps_g = p.psum(f"ffn_g_{tag}", [128, 512], F32)
    ps_u = p.psum(f"ffn_u_{tag}", [128, 512], F32)
    ps_d = p.psum(f"ffn_d_{tag}", [128, 1024], F32)
    p.start_after = p.frontier()
    nblk = [(i * 512, min(512, FFN_I - i * 512)) for i in range(6)]
    for it in range(T // 128):
        tl = p.dma("qs", x_tm[:], x_in[ts(it, 128), :])
        t_ln = emit_layernorm_tile(p, x_tm[:], ln_tm[:], D, after=[tl],
                                   stats=lnb)
        t_q = emit_quant(p, ln_tm[:], xq_tm[:], qb, nf=D, after=[t_ln])
        trt = None
        for c in range(8):
            trt = p.op("t", "transpose", ps_tr[:, ts(c, 128)],
                       xq_tm[:, ts(c, 128)], identb[:], after=[t_q])
        t_xf = p.op("v", "tensor_copy", xq_fm[:], ps_tr[:, 0:1024],
                    after=[trt])
        h_toks = []
        blk_prev = []
        for (n0, w) in nblk:
            t_mm = None
            for c in range(8):
                st = (c == 0)
                sp = (c == 7)
                p.raw("t", "matmul", ps_g[:, 0:w], xq_fm[:, ts(c, 128)],
                      wgqT[:, c * FFN_I + n0:c * FFN_I + n0 + w],
                      start=st, stop=sp, after=[t_xf] + blk_prev)
                if not sp:
                    p.raw("t", "matmul", ps_u[:, 0:w], xq_fm[:, ts(c, 128)],
                          wuqT[:, c * FFN_I + n0:c * FFN_I + n0 + w],
                          start=st, stop=sp)
                else:
                    t_mm = p.op("t", "matmul", ps_u[:, 0:w],
                                xq_fm[:, ts(c, 128)],
                                wuqT[:, c * FFN_I + n0:c * FFN_I + n0 + w],
                                start=st, stop=sp)
            t_sil = p.op("a", "activation", sil[:, 0:w], ps_g[:, 0:w],
                         AF.Silu, after=[t_mm])
            ht = p.op("v", "tensor_tensor", h_tm[:, n0:n0 + w],
                      sil[:, 0:w], ps_u[:, 0:w], ALU.mult,
                      after=[t_sil, t_mm])
            h_toks.append(ht)
            blk_prev = [t_sil, ht]
        t_hq = emit_quant(p, h_tm[:], hq_tm[:], qb, nf=FFN_I, after=h_toks)
        grp_mm_tok = []
        for cg in range(0, 22, 8):
            ng = min(8, 22 - cg)
            trt = None
            for c in range(cg, cg + ng):
                trt = p.op("t", "transpose", ps_tr[:, ts(c - cg, 128)],
                           hq_tm[:, ts(c, 128)], identb[:], after=[t_hq])
            t_hf = p.op("v", "tensor_copy", hq_fm[:, 0:ng * 128],
                        ps_tr[:, 0:ng * 128], after=[trt] + grp_mm_tok)
            last = None
            for n2 in range(2):
                for c in range(ng):
                    fn = p.raw if not (n2 == 1 and c == ng - 1) else p.op
                    last = fn("t", "matmul", ps_d[:, ts(n2, 512)],
                              hq_fm[:, ts(c, 128)],
                              wdqT[:, (cg + c) * D + n2 * 512:
                                   (cg + c) * D + n2 * 512 + 512],
                              start=(cg == 0 and c == 0),
                              stop=(cg == 16 and c == ng - 1), after=[t_hf])
            grp_mm_tok = [last]
        t_md = grp_mm_tok[0]
        r0 = p.op("v", "tensor_tensor", x_tm[:, 0:512], x_tm[:, 0:512],
                  ps_d[:, 0:512], ALU.add, after=[t_md])
        r1 = p.op("v", "tensor_tensor", x_tm[:, 512:1024], x_tm[:, 512:1024],
                  ps_d[:, 512:1024], ALU.add, after=[t_md])
        st = p.dma("qs", x_out[ts(it, 128), :], x_tm[:], after=[r0, r1])
        p.start_after = p.frontier()
    return p.frontier()


import os as _os
S7_STAGE = int(_os.environ.get("S7_STAGE", "9"))

def emit_attn_causal(p, x_in, x_out, wq_ap, wk_ap, wv_ap, lnb, identb, tag):
    T = x_in.shape[0]
    x_tm = p.sbuf(f"at_x_{tag}", [128, D], F32)
    ln_tm = p.sbuf(f"at_ln_{tag}", [128, D], BF16)
    ln_fm = p.sbuf(f"at_lnf_{tag}", [128, D], BF16)
    kfm = p.sbuf(f"at_k_{tag}", [128, 8 * T], BF16)
    vtm = p.sbuf(f"at_v_{tag}", [128, 32 * D], BF16)
    wkv = p.sbuf(f"at_w_{tag}", [128, 2 * 8 * D], BF16)
    qfm = p.sbuf(f"at_q_{tag}", [128, 8 * 128], BF16)
    sc = p.sbuf(f"at_sc_{tag}", [128, T], F32)
    pbf = p.sbuf(f"at_p_{tag}", [128, T], BF16)
    ptr_sb = p.sbuf(f"at_ptr_{tag}", [128, 128], BF16)
    rmax = p.sbuf(f"at_rm_{tag}", [128, 1], F32)
    rsum = p.sbuf(f"at_rs_{tag}", [128, 1], F32)
    rinv = p.sbuf(f"at_ri_{tag}", [128, 1], F32)
    o_tm = p.sbuf(f"at_o_{tag}", [128, D], F32)
    ps_tr = p.psum(f"at_tr_{tag}", [128, 1024], BF16)
    ps_k = p.psum(f"at_pk_{tag}", [128, 1024], F32)
    ps_v = p.psum(f"at_pv_{tag}", [128, 1024], F32)
    ps_s = ps_k
    ps_q = ps_v
    ps_ptr = p.psum(f"at_ppt_{tag}", [128, 128], BF16)
    ps_o = p.psum(f"at_po_{tag}", [128, 1024], F32)

    p.start_after = p.frontier()
    tw = [p.dma("qp", wkv[:, 0:8 * D].rearrange("p (c i) -> p c i", c=8),
                wk_ap.rearrange("(c p) i -> p c i", p=128)),
          p.dma("qp", wkv[:, 8 * D:16 * D].rearrange("p (c i) -> p c i", c=8),
                wv_ap.rearrange("(c p) i -> p c i", p=128))]
    for g in range(T // 512):
        lnf_toks = []
        for s4 in range(4):
            it = g * 4 + s4
            tl = p.dma("qs", x_tm[:], x_in[ts(it, 128), :])
            t_ln = emit_layernorm_tile(p, x_tm[:], ln_tm[:], D, after=[tl],
                                       stats=lnb)
            trt = None
            for c in range(8):
                trt = p.op("t", "transpose", ps_tr[:, ts(c, 128)],
                           ln_tm[:, ts(c, 128)], identb[:], after=[t_ln])
            t_lf = p.op("v", "tensor_copy", ln_fm[:], ps_tr[:, 0:1024],
                        after=[trt])
            if S7_STAGE <= 0:
                p.start_after = p.frontier()
                continue
            for n2 in range(2):
                mm = None
                for c in range(8):
                    fn = p.op if c == 7 else p.raw
                    mm = fn("t", "matmul", ps_v[:, ts(n2, 512)],
                            ln_fm[:, ts(c, 128)],
                            wkv[:, 8 * D + c * D + n2 * 512:
                                 8 * D + c * D + n2 * 512 + 512],
                            start=(c == 0), stop=(c == 7),
                            after=[t_lf] + tw)
                p.op("v", "tensor_copy",
                     vtm[:, it * D + n2 * 512: it * D + n2 * 512 + 512],
                     ps_v[:, ts(n2, 512)], after=[mm])
            mm = None
            for ic in range(8):
                for c in range(8):
                    fn = p.op if (c == 7 and ic == 7) else p.raw
                    mm = fn("t", "matmul", ps_k[:, ts(ic, 128)],
                            wkv[:, c * D + ic * 128: c * D + ic * 128 + 128],
                            ln_fm[:, ts(c, 128)],
                            start=(c == 0), stop=(c == 7), after=[t_lf] + tw)
            kdst = bass.AP(kfm[:].tensor, kfm[:].offset + it * 128,
                           [list(kfm[:].ap[0]), [T, 8], [1, 128]])
            psrc = ps_k[:, 0:1024].rearrange("p (i q) -> p i q", i=8)
            p.op("v", "tensor_copy", kdst, psrc, after=[mm])
            p.start_after = p.frontier()
    tq = p.dma("qp", wkv[:, 0:8 * D].rearrange("p (c i) -> p c i", c=8),
               wq_ap.rearrange("(c p) i -> p c i", p=128),
               after=p.frontier())
    p.start_after = p.frontier()
    inv_sqrt_d = 1.0 / 32.0
    for qi in range(T // 128):
        tl = p.dma("qs", x_tm[:], x_in[ts(qi, 128), :])
        t_ln = emit_layernorm_tile(p, x_tm[:], ln_tm[:], D, after=[tl],
                                   stats=lnb)
        trt = None
        for c in range(8):
            trt = p.op("t", "transpose", ps_tr[:, ts(c, 128)],
                       ln_tm[:, ts(c, 128)], identb[:], after=[t_ln])
        t_lf = p.op("v", "tensor_copy", ln_fm[:], ps_tr[:, 0:1024],
                    after=[trt])
        if S7_STAGE <= 1:
            st = p.op("v", "tensor_copy", x_tm[:], ln_fm[:], after=[t_lf])
            p.dma("qs", x_out[ts(qi, 128), :], x_tm[:], after=[st])
            p.start_after = p.frontier()
            continue
        mm = None
        for ic in range(8):
            for c in range(8):
                fn = p.op if (c == 7 and ic == 7) else p.raw
                mm = fn("t", "matmul", ps_q[:, ts(ic, 128)],
                        wkv[:, c * D + ic * 128: c * D + ic * 128 + 128],
                        ln_fm[:, ts(c, 128)],
                        start=(c == 0), stop=(c == 7), after=[t_lf, tq])
        qtoks = [p.op("v", "tensor_copy", qfm[:], ps_q[:, 0:1024],
                      after=[mm])]
        kmax = (qi + 1) * 128
        sc_toks = []
        for kb in range((kmax + 511) // 512):
            w = min(512, kmax - kb * 512)
            mm = None
            for c in range(8):
                fn = p.op if c == 7 else p.raw
                mm = fn("t", "matmul", ps_s[:, 0:w], qfm[:, ts(c, 128)],
                        kfm[:, c * T + kb * 512: c * T + kb * 512 + w],
                        start=(c == 0), stop=(c == 7),
                        after=qtoks + sc_toks)
            sc_toks = [p.op("v", "tensor_scalar_mul",
                            sc[:, kb * 512:kb * 512 + w], ps_s[:, 0:w],
                            inv_sqrt_d, after=[mm])]
        if S7_STAGE <= 2:
            st = p.op("v", "tensor_copy", x_tm[:, 0:512], sc[:, 0:512],
                      after=sc_toks)
            p.dma("qs", x_out[ts(qi, 128), :], x_tm[:], after=[st])
            p.start_after = p.frontier()
            continue
        t_cm = p.op("p", "affine_select", sc[:, kmax - 128:kmax],
                    sc[:, kmax - 128:kmax], pattern=[[-1, 128]],
                    compare_op=ALU.is_ge, fill=-1e30, base=0,
                    channel_multiplier=1, after=sc_toks)
        t_m = p.op("v", "reduce_max", rmax[:], sc[:, 0:kmax], AX.X,
                   after=[t_cm] + sc_toks)
        t_nm = p.op("v", "tensor_scalar_mul", rmax[:], rmax[:], -1.0,
                    after=[t_m])
        t_e = p.op("a", "activation", pbf[:, 0:kmax], sc[:, 0:kmax], AF.Exp,
                   bias=rmax[:], after=[t_nm])
        t_s = p.op("v", "reduce_sum", rsum[:], pbf[:, 0:kmax], AX.X,
                   after=[t_e])
        t_ri = p.op("v", "reciprocal", rinv[:], rsum[:], after=[t_s])
        if S7_STAGE <= 3:
            st = p.op("v", "tensor_copy", x_tm[:, 0:512], pbf[:, 0:512],
                      after=[t_ri, t_e])
            p.dma("qs", x_out[ts(qi, 128), :], x_tm[:], after=[st])
            p.start_after = p.frontier()
            continue
        prev_mm = []
        for kc in range(qi + 1):
            t_tr = p.op("t", "transpose", ps_ptr[:, 0:128],
                        pbf[:, ts(kc, 128)],
                        identb[:], after=[t_e] + prev_mm)
            t_ev = p.op("v", "tensor_copy", ptr_sb[:], ps_ptr[:, 0:128],
                        after=[t_tr] + prev_mm)
            prev_mm = []
            for n2 in range(2):
                mm = p.op("t", "matmul", ps_o[:, ts(n2, 512)], ptr_sb[:],
                          vtm[:, kc * D + n2 * 512: kc * D + n2 * 512 + 512],
                          start=(kc == 0), stop=(kc == qi), after=[t_ev])
                prev_mm.append(mm)
        r0 = p.op("v", "tensor_scalar_mul", o_tm[:, 0:512], ps_o[:, 0:512],
                  rinv[:], after=prev_mm + [t_ri])
        r1 = p.op("v", "tensor_scalar_mul", o_tm[:, 512:1024],
                  ps_o[:, 512:1024], rinv[:], after=prev_mm + [t_ri])
        r2 = p.op("v", "tensor_add", x_tm[:], x_tm[:], o_tm[:],
                  after=[r0, r1])
        p.dma("qs", x_out[ts(qi, 128), :], x_tm[:], after=[r2])
        p.start_after = p.frontier()
    return p.frontier()


def emit_ffn_l1(p, x_in, x_out, rw_ap, w1_ap, w2_ap, sgu_ap, sdn_ap, sgw_ap,
                bufs, lnb, identb, tag):
    T_ = x_in.shape[0]
    b = bufs
    p.start_after = p.frontier()
    tshared = [
        p.dma("qp", b["sgu"][:].rearrange("p (c i) -> p c i", c=8),
              sgu_ap.rearrange("(c p) i -> p c i", p=128)),
        p.dma("qp", b["sdn"][:].rearrange("p (c i) -> p c i", c=8),
              sdn_ap.rearrange("(c p) i -> p c i", p=128)),
        p.dma("qp", b["sgw"][:].rearrange("p (c i) -> p c i", c=8),
              sgw_ap.rearrange("(c p) i -> p c i", p=128),
              allow_slow_non_contiguous=True),
        p.dma("qp", b["rw"][:].rearrange("p (c i) -> p c i", c=8),
              rw_ap.rearrange("(c p) i -> p c i", p=128)),
    ]
    p.start_after = p.frontier()
    for g in range(T_ // 512):
        ln_toks = []
        rtr_prev = []
        for s4 in range(4):
            it = g * 4 + s4
            tl = p.dma("qs", b["x_tm"][:, ts(s4, 1024)],
                       x_in[ts(it, 128), :])
            t_ln = emit_layernorm_tile(p, b["x_tm"][:, ts(s4, 1024)],
                                       b["ln_tm"][:], D, after=[tl],
                                       stats=lnb)
            trt = None
            for c in range(8):
                trt = p.op("t", "transpose", b["ps_tr"][:, ts(c, 128)],
                           b["ln_tm"][:, ts(c, 128)], identb[:],
                           after=[t_ln])
            lnf3 = bass.AP(b["ln_fm"][:].tensor,
                           b["ln_fm"][:].offset + s4 * 128,
                           [list(b["ln_fm"][:].ap[0]), [512, 8], [1, 128]])
            t_lf = p.op("v", "tensor_copy", lnf3,
                        b["ps_tr"][:, 0:1024].rearrange(
                            "p (c q) -> p c q", c=8), after=[trt])
            ln_toks.append(t_lf)
            mm = None
            for c in range(8):
                fn = p.op if c == 7 else p.raw
                mm = fn("t", "matmul", b["ps_lg"][:, 0:8],
                        b["ln_fm"][:, c * 512 + s4 * 128:
                                    c * 512 + s4 * 128 + 128],
                        b["rw"][:, c * 8:c * 8 + 8],
                        start=(c == 0), stop=(c == 7),
                        after=[t_lf] + tshared + rtr_prev)
            t_m = p.op("v", "reduce_max", b["r1"][:], b["ps_lg"][:, 0:8],
                       AX.X, after=[mm])
            t_nm = p.op("v", "tensor_scalar_mul", b["r1"][:], b["r1"][:],
                        -1.0, after=[t_m])
            t_e = p.op("a", "activation", b["pr"][:], b["ps_lg"][:, 0:8],
                       AF.Exp, bias=b["r1"][:], after=[t_nm, mm])
            t_s = p.op("v", "reduce_sum", b["r2"][:], b["pr"][:], AX.X,
                       after=[t_e])
            t_ri = p.op("v", "reciprocal", b["r3"][:], b["r2"][:],
                        after=[t_s])
            t_p = p.op("v", "tensor_scalar_mul", b["pr"][:], b["pr"][:],
                       b["r3"][:], after=[t_ri, t_e])
            t_m1 = p.op("v", "reduce_max", b["r1"][:], b["pr"][:], AX.X,
                        after=[t_p])
            t_q1 = p.op("v", "tensor_scalar", b["e1"][:], b["pr"][:],
                        b["r1"][:], None, ALU.is_ge, after=[t_m1, t_p])
            t_mk = p.op("v", "scalar_tensor_tensor", b["e2"][:], b["e1"][:],
                        -2.0, b["pr"][:], ALU.mult, ALU.add,
                        after=[t_q1, t_p])
            t_m2 = p.op("v", "reduce_max", b["r2"][:], b["e2"][:], AX.X,
                        after=[t_mk])
            t_k = p.op("v", "tensor_scalar", b["e1"][:], b["pr"][:],
                       b["r2"][:], None, ALU.is_ge, after=[t_m2, t_p])
            p.op("v", "tensor_tensor", b["cm"][:, ts(s4, 8)], b["pr"][:],
                 b["e1"][:], ALU.mult, after=[t_k])
            rtr_prev = [t_e, t_m]
        p.start_after = p.frontier()
        acc_tok = [p.op("v", "memset", b["acc"][:], 0.0)]
        def _wload(e, after):
            sl = e % 2
            return [
                p.dma("qp", b["w1e"][sl][:].rearrange("p (c i) -> p c i",
                                                      c=8),
                      w1_ap[e].rearrange("(c p) i -> p c i", p=128),
                      after=after),
                p.dma("qp", b["w2e"][sl][:].rearrange("p (c i) -> p c i",
                                                      c=8),
                      w2_ap[e].rearrange("(c p) i -> p c i", p=128),
                      after=after),
            ]
        wtok = {0: _wload(0, []), 1: _wload(1, [])}
        h_done = []
        for e in range(8):
            sl = e % 2
            sil_toks = []
            for fh in range(2):
                mm = None
                for fc in range(4):
                    f = fh * 4 + fc
                    for c in range(8):
                        fn = p.op if (c == 7 and fc == 3) else p.raw
                        mm = fn("t", "matmul", b["ps_h"][:, ts(fc, 512)],
                                b["w1e"][sl][:, c * 1024 + f * 128:
                                             c * 1024 + f * 128 + 128],
                                b["ln_fm"][:, ts(c, 512)],
                                start=(c == 0), stop=(c == 7),
                                after=wtok[e] + ln_toks + h_done + sil_toks)
                sil_toks.append(
                    p.op("a", "activation",
                         b["h_fm"][:].rearrange("p (f q) -> p f q", f=8)
                         if False else
                         bass.AP(b["h_fm"][:].tensor,
                                 b["h_fm"][:].offset + fh * 4 * 512,
                                 [list(b["h_fm"][:].ap[0]), [1, 4 * 512]]),
                         b["ps_h"][:, 0:2048], AF.Silu, after=[mm]))
            h_done = []
            for s4 in range(4):
                mm2 = None
                for n2 in range(2):
                    for f in range(8):
                        fn = p.op if (f == 7 and n2 == 1) else p.raw
                        mm2 = fn("t", "matmul", b["ps_r"][:, ts(n2, 512)],
                                 b["h_fm"][:, f * 512 + s4 * 128:
                                            f * 512 + s4 * 128 + 128],
                                 b["w2e"][sl][:, f * 1024 + n2 * 512:
                                              f * 1024 + n2 * 512 + 512],
                                 start=(f == 0), stop=(f == 7),
                                 after=sil_toks + acc_tok + h_done)
                h_done.append(
                    p.op("v", "scalar_tensor_tensor",
                         b["acc"][:, ts(s4, 1024)], b["ps_r"][:, 0:1024],
                         b["cm"][:, s4 * 8 + e: s4 * 8 + e + 1],
                         b["acc"][:, ts(s4, 1024)], ALU.mult, ALU.add,
                         after=[mm2]))
            if e + 2 < 8:
                wtok[e + 2] = _wload(e + 2, h_done[-1:])
        sh_done = list(h_done)
        for s4 in range(4):
            mmg = mmu = None
            for c in range(8):
                lhs = b["ln_fm"][:, c * 512 + s4 * 128:
                                 c * 512 + s4 * 128 + 128]
                for n2 in range(2):
                    p.raw("t", "matmul", b["ps_h"][:, ts(n2, 512)], lhs,
                          b["sgu"][:, c * 2048 + n2 * 512:
                                   c * 2048 + n2 * 512 + 512],
                          start=(c == 0), stop=(c == 7),
                          after=sh_done + tshared)
                for n2 in range(2):
                    fn = p.op if (c == 7 and n2 == 1) else p.raw
                    mmu = fn("t", "matmul", b["ps_h"][:, ts(2 + n2, 512)],
                             lhs,
                             b["sgu"][:, c * 2048 + 1024 + n2 * 512:
                                      c * 2048 + 1024 + n2 * 512 + 512],
                             start=(c == 0), stop=(c == 7))
            mm3 = None
            for c in range(8):
                fn = p.op if c == 7 else p.raw
                mm3 = fn("t", "matmul", b["ps_lg"][:, 0:1],
                         b["ln_fm"][:, c * 512 + s4 * 128:
                                     c * 512 + s4 * 128 + 128],
                         b["sgw"][:, c * 1: c * 1 + 1],
                         start=(c == 0), stop=(c == 7), after=[mmu])
            t_sg = p.op("a", "activation", b["r1"][:], b["ps_lg"][:, 0:1],
                        AF.Sigmoid, after=[mm3])
            t_si = p.op("a", "activation", b["sil2"][:, 0:1024],
                        b["ps_h"][:, 0:1024], AF.Silu, after=[mmu])
            t_h = p.op("v", "tensor_tensor", b["sh_tm"][:],
                       b["sil2"][:, 0:1024], b["ps_h"][:, 1024:2048],
                       ALU.mult, after=[t_si, mmu])
            trt = None
            for c in range(8):
                trt = p.op("t", "transpose", b["ps_tr"][:, ts(c, 128)],
                           b["sh_tm"][:, ts(c, 128)], identb[:],
                           after=[t_h])
            t_hf = p.op("v", "tensor_copy", b["sh_fm"][:],
                        b["ps_tr"][:, 0:1024], after=[trt])
            mm4 = None
            for n2 in range(2):
                for c in range(8):
                    fn = p.op if (c == 7 and n2 == 1) else p.raw
                    mm4 = fn("t", "matmul", b["ps_r"][:, ts(n2, 512)],
                             b["sh_fm"][:, ts(c, 128)],
                             b["sdn"][:, c * 1024 + n2 * 512:
                                      c * 1024 + n2 * 512 + 512],
                             start=(c == 0), stop=(c == 7), after=[t_hf])
            sh_done = [p.op("v", "scalar_tensor_tensor",
                            b["acc"][:, ts(s4, 1024)], b["ps_r"][:, 0:1024],
                            b["r1"][:], b["acc"][:, ts(s4, 1024)],
                            ALU.mult, ALU.add, after=[mm4, t_sg])]
        for s4 in range(4):
            it = g * 4 + s4
            r2 = p.op("v", "tensor_add", b["x_tm"][:, ts(s4, 1024)],
                      b["x_tm"][:, ts(s4, 1024)], b["acc"][:, ts(s4, 1024)],
                      after=sh_done)
            p.dma("qs", x_out[ts(it, 128), :], b["x_tm"][:, ts(s4, 1024)],
                  after=[r2])
        p.start_after = p.frontier()
    return p.frontier()


def ffn_l1_bufs(p, tag):
    return {
        "x_tm": p.sbuf(f"l1_x_{tag}", [128, 4 * D], F32),
        "ln_tm": p.sbuf(f"l1_ln_{tag}", [128, D], BF16),
        "ln_fm": p.sbuf(f"l1_lnf_{tag}", [128, 8 * 512], BF16),
        "h_fm": p.sbuf(f"l1_h_{tag}", [128, 8 * 512], BF16),
        "sh_tm": p.sbuf(f"l1_sht_{tag}", [128, D], BF16),
        "sh_fm": p.sbuf(f"l1_shf_{tag}", [128, D], BF16),
        "sil2": p.sbuf(f"l1_sil_{tag}", [128, 1024], F32),
        "acc": p.sbuf(f"l1_acc_{tag}", [128, 4 * D], F32),
        "cm": p.sbuf(f"l1_cm_{tag}", [128, 32], F32),
        "pr": p.sbuf(f"l1_pr_{tag}", [128, 8], F32),
        "e1": p.sbuf(f"l1_e1_{tag}", [128, 8], F32),
        "e2": p.sbuf(f"l1_e2_{tag}", [128, 8], F32),
        "r1": p.sbuf(f"l1_r1_{tag}", [128, 1], F32),
        "r2": p.sbuf(f"l1_r2_{tag}", [128, 1], F32),
        "r3": p.sbuf(f"l1_r3_{tag}", [128, 1], F32),
        "rw": p.sbuf(f"l1_rw_{tag}", [128, 8 * 8], BF16),
        "sgu": p.sbuf(f"l1_sgu_{tag}", [128, 8 * 2048], BF16),
        "sdn": p.sbuf(f"l1_sdn_{tag}", [128, 8 * 1024], BF16),
        "sgw": p.sbuf(f"l1_sgw_{tag}", [128, 8 * 1], BF16),
        "w1e": [p.sbuf(f"l1_w1a_{tag}", [128, 8 * 1024], BF16),
                p.sbuf(f"l1_w1b_{tag}", [128, 8 * 1024], BF16)],
        "w2e": [p.sbuf(f"l1_w2a_{tag}", [128, 8 * 1024], BF16),
                p.sbuf(f"l1_w2b_{tag}", [128, 8 * 1024], BF16)],
        "ps_tr": p.psum(f"l1_tr_{tag}", [128, 1024], BF16),
        "ps_lg": p.psum(f"l1_lg_{tag}", [128, 8], F32),
        "ps_h": p.psum(f"l1_psh_{tag}", [128, 4 * 512], F32),
        "ps_r": p.psum(f"l1_r_{tag}", [128, 1024], F32),
    }


def emit_rwkv(p, y_in, o_out, wk_ap, wv_ap, wout_ap, tmrkv_ap, td_ap, tf_ap,
              identf, tag):
    TS = y_in.shape[1]
    nt = TS // 128
    ncs = TS // 512
    W = TS
    yA = p.sbuf(f"rw_ya_{tag}", [128, 8 * W], BF16)
    yB = p.sbuf(f"rw_yb_{tag}", [128, 8 * W], BF16)
    kin = p.sbuf(f"rw_kin_{tag}", [128, 8 * W], BF16)
    vin = p.sbuf(f"rw_vin_{tag}", [128, 8 * W], BF16)
    kc = p.sbuf(f"rw_k_{tag}", [128, 8 * W], F32)
    vc = p.sbuf(f"rw_v_{tag}", [128, 8 * W], F32)
    Ab = p.sbuf(f"rw_A_{tag}", [128, 8 * W], F32)
    Bb = p.sbuf(f"rw_B_{tag}", [128, 8 * W], F32)
    Pm = p.sbuf(f"rw_P_{tag}", [128, 8 * W], F32)
    rbf = p.sbuf(f"rw_r_{tag}", [128, 8 * W], BF16)
    sc1 = p.sbuf(f"rw_s1_{tag}", [128, W], F32)
    sc2 = p.sbuf(f"rw_s2_{tag}", [128, W], F32)
    sc3 = p.sbuf(f"rw_s3_{tag}", [128, W], F32)
    sc4 = p.sbuf(f"rw_s4_{tag}", [128, W], F32)
    x_tm = p.sbuf(f"rw_x_{tag}", [128, D], F32)
    o_tm = p.sbuf(f"rw_o_{tag}", [128, D], F32)
    wkb = p.sbuf(f"rw_wk_{tag}", [128, 8 * D], BF16)
    wvb = p.sbuf(f"rw_wv_{tag}", [128, 8 * D], BF16)
    wob = p.sbuf(f"rw_wo_{tag}", [128, 8 * D], BF16)
    mkv = p.sbuf(f"rw_mkv_{tag}", [128, 8 * 4], F32)
    ps_tr = p.psum(f"rw_tr_{tag}", [128, 1024], F32)
    ps_k = p.psum(f"rw_pk_{tag}", [128, 4 * 512], F32)
    ps_o = p.psum(f"rw_po_{tag}", [128, 1024], F32)

    p.start_after = p.frontier()
    tw = [p.dma("qp", wkb[:].rearrange("p (c i) -> p c i", c=8),
                wk_ap.rearrange("(c p) i -> p c i", p=128)),
          p.dma("qp", wvb[:].rearrange("p (c i) -> p c i", c=8),
                wv_ap.rearrange("(c p) i -> p c i", p=128)),
          p.dma("qp", wob[:].rearrange("p (c i) -> p c i", c=8),
                wout_ap.rearrange("(c p) i -> p c i", p=128)),
          p.dma("qs", mkv[:].rearrange("p (c i) -> p c i", c=8)[:, :, 0:1],
                tmrkv_ap[D:2 * D].rearrange("(c p one) -> p c one", p=128, one=1),
                allow_slow_non_contiguous=True),
          p.dma("qs", mkv[:].rearrange("p (c i) -> p c i", c=8)[:, :, 1:2],
                tmrkv_ap[2 * D:3 * D].rearrange("(c p one) -> p c one", p=128, one=1),
                allow_slow_non_contiguous=True),
          p.dma("qs", mkv[:].rearrange("p (c i) -> p c i", c=8)[:, :, 2:3],
                td_ap.rearrange("(c p one) -> p c one", p=128, one=1),
                allow_slow_non_contiguous=True),
          p.dma("qs", mkv[:].rearrange("p (c i) -> p c i", c=8)[:, :, 3:4],
                tf_ap.rearrange("(c p one) -> p c one", p=128, one=1),
                allow_slow_non_contiguous=True)]
    t_uw = p.op("v", "tensor_tensor",
                mkv[:].rearrange("p (c i) -> p c i", c=8)[:, :, 3:4],
                mkv[:].rearrange("p (c i) -> p c i", c=8)[:, :, 3:4],
                mkv[:].rearrange("p (c i) -> p c i", c=8)[:, :, 2:3],
                ALU.add, after=tw)
    z0 = p.op("v", "memset", yB[:], 0.0)
    p.start_after = p.frontier()
    ysl = [yA, yB]
    out_toks = []
    for s in range(8):
        y_cur, y_prev = ysl[s % 2], ysl[1 - s % 2]
        prev_tr, prev_ev = [], []
        for i4 in range(nt):
            tl = p.dma("qs", x_tm[:], y_in[s, ts(i4, 128), :],
                       after=prev_tr)
            trt = None
            for c in range(8):
                trt = p.op("t", "transpose", ps_tr[:, ts(c, 128)],
                           x_tm[:, ts(c, 128)], identf[:],
                           after=[tl] + prev_ev)
            ydst = bass.AP(y_cur[:].tensor, y_cur[:].offset + i4 * 128,
                           [list(y_cur[:].ap[0]), [W, 8], [1, 128]])
            prev_ev = [p.op("v", "tensor_copy", ydst,
                            ps_tr[:, 0:1024].rearrange("p (c q) -> p c q",
                                                       c=8),
                            after=[trt])]
            prev_tr = [trt]
        t_y = p.frontier()
        mix_toks = []
        for c in range(8):
            mk_c = mkv[:, c * 4 + 0:c * 4 + 1]
            mv_c = mkv[:, c * 4 + 1:c * 4 + 2]
            t_d = p.op("v", "tensor_sub", sc1[:], y_cur[:, ts(c, W)],
                       y_prev[:, ts(c, W)], after=t_y + [t_uw, z0])
            mix_toks.append(p.op("v", "scalar_tensor_tensor",
                                 kin[:, ts(c, W)], sc1[:], mk_c,
                                 y_prev[:, ts(c, W)], ALU.mult, ALU.add,
                                 after=[t_d]))
            mix_toks.append(p.op("v", "scalar_tensor_tensor",
                                 vin[:, ts(c, W)], sc1[:], mv_c,
                                 y_prev[:, ts(c, W)], ALU.mult, ALU.add,
                                 after=[t_d]))
        prj_toks = []
        for (src, dstb, wsb) in ((kin, kc, wkb), (vin, vc, wvb)):
            for tg in range(ncs):
                mm = None
                for ib in range(2):
                    for ic4 in range(4):
                        ic = ib * 4 + ic4
                        for c in range(8):
                            fn = p.op if (c == 7 and ic4 == 3) else p.raw
                            mm = fn("t", "matmul", ps_k[:, ts(ic4, 512)],
                                    wsb[:, c * D + ic * 128:
                                        c * D + ic * 128 + 128],
                                    src[:, c * W + tg * 512:
                                        c * W + tg * 512 + 512],
                                    start=(c == 0), stop=(c == 7),
                                    after=mix_toks + prj_toks)
                    for ic4 in range(4):
                        ic = ib * 4 + ic4
                        prj_toks.append(
                            p.op("v", "tensor_copy",
                                 dstb[:, ic * W + tg * 512:
                                      ic * W + tg * 512 + 512],
                                 ps_k[:, ts(ic4, 512)], after=[mm]))
        emit_toks = []
        for c in range(8):
            W_c = mkv[:, c * 4 + 2:c * 4 + 3]
            uW_c = mkv[:, c * 4 + 3:c * 4 + 4]
            A_c, B_c, P_c = Ab[:, ts(c, W)], Bb[:, ts(c, W)], Pm[:, ts(c, W)]
            k_c, v_c = kc[:, ts(c, W)], vc[:, ts(c, W)]
            if s == 0:
                t1 = p.op("v", "tensor_copy", A_c, v_c, after=prj_toks)
                t2 = p.op("v", "memset", B_c, 1.0, after=prj_toks)
                t3 = p.op("v", "tensor_copy", P_c, k_c, after=prj_toks)
                st_toks = [t1, t2, t3]
            else:
                t1 = p.op("v", "tensor_scalar", sc1[:], P_c, W_c, None,
                          ALU.add, after=prj_toks + scan_prev + emit_toks)
                t2 = p.op("v", "tensor_max", P_c, sc1[:], k_c, after=[t1])
                t3 = p.op("v", "tensor_sub", sc1[:], sc1[:], P_c, after=[t2])
                t4 = p.op("a", "activation", sc1[:], sc1[:], AF.Exp,
                          after=[t3])
                t5 = p.op("v", "tensor_sub", sc2[:], k_c, P_c, after=[t2])
                t6 = p.op("a", "activation", sc2[:], sc2[:], AF.Exp,
                          after=[t5])
                t7 = p.op("v", "tensor_mul", sc3[:], v_c, sc2[:], after=[t6])
                t8 = p.op("v", "tensor_mul", A_c, A_c, sc1[:], after=[t4, t7])
                t9 = p.op("v", "tensor_add", A_c, A_c, sc3[:], after=[t8])
                ta = p.op("v", "tensor_mul", B_c, B_c, sc1[:], after=[t4])
                tb = p.op("v", "tensor_add", B_c, B_c, sc2[:], after=[ta])
                st_toks = [t9, tb]
            e1 = p.op("v", "tensor_scalar", sc2[:], k_c, uW_c, None, ALU.add,
                      after=st_toks + emit_toks)
            e2 = p.op("v", "tensor_max", sc3[:], P_c, sc2[:], after=[e1])
            e3 = p.op("v", "tensor_sub", sc4[:], P_c, sc3[:], after=[e2])
            e4 = p.op("a", "activation", sc4[:], sc4[:], AF.Exp, after=[e3])
            e5 = p.op("v", "tensor_sub", sc2[:], sc2[:], sc3[:], after=[e2])
            e6 = p.op("a", "activation", sc2[:], sc2[:], AF.Exp, after=[e5])
            e7 = p.op("v", "tensor_mul", sc3[:], v_c, sc2[:], after=[e6])
            e8 = p.op("v", "tensor_mul", sc1[:], A_c, sc4[:], after=[e4, e7])
            e9 = p.op("v", "tensor_add", sc1[:], sc1[:], sc3[:], after=[e8])
            ea = p.op("v", "tensor_mul", sc4[:], B_c, sc4[:], after=[e4, e8])
            eb = p.op("v", "tensor_add", sc4[:], sc4[:], sc2[:], after=[ea])
            ec = p.op("v", "reciprocal", sc4[:], sc4[:], after=[eb])
            emit_toks.append(p.op("v", "tensor_mul", rbf[:, ts(c, W)],
                                  sc1[:], sc4[:], after=[ec, e9]))
        scan_prev = emit_toks
        for i4 in range(nt):
            mm = None
            for n2 in range(2):
                for c in range(8):
                    fn = p.op if (c == 7 and n2 == 1) else p.raw
                    mm = fn("t", "matmul", ps_o[:, ts(n2, 512)],
                            rbf[:, c * W + i4 * 128: c * W + i4 * 128 + 128],
                            wob[:, c * D + n2 * 512: c * D + n2 * 512 + 512],
                            start=(c == 0), stop=(c == 7),
                            after=emit_toks + out_toks)
            ev = p.op("v", "tensor_copy", o_tm[:], ps_o[:, 0:1024],
                      after=[mm])
            out_toks = [p.dma("qs", o_out[s, ts(i4, 128), :], o_tm[:],
                              after=[ev]), ev]
        p.start_after = p.frontier()
    return p.frontier()


def emit_linattn(p, x_in, base3_out, ln2_out, idx1_o, idx2_o, g1_o, g2_o,
                 rk1_o, rk2_o, tot_o, weights, lnb, identb, identf, tag):
    T_ = x_in.shape[0]
    H, DK = 16, 64
    w = weights
    wsb = {}
    for nm in ("wq", "wk", "wv", "wg", "wo"):
        wsb[nm] = p.sbuf(f"la_{nm}_{tag}", [128, 8 * D], BF16)
    bb = {}
    for nm in ("bq", "bk", "bv", "bo"):
        bb[nm] = p.sbuf(f"la_{nm}_{tag}", [128, D], BF16)
    bg_fm = p.sbuf(f"la_bgf_{tag}", [128, 8], F32)
    s1_b = p.sbuf(f"la_s1_{tag}", [128, D], BF16)
    s2_b = p.sbuf(f"la_s2_{tag}", [128, D], F32)
    rw_sb = p.sbuf(f"la_rw_{tag}", [128, 8 * 8], F32)
    rb_b = p.sbuf(f"la_rb_{tag}", [128, 8], F32)
    iota8 = p.sbuf(f"la_io_{tag}", [128, 8], F32)
    e2c = p.sbuf(f"la_e2_{tag}", [2, 128], F32)
    ones64 = p.sbuf(f"la_on_{tag}", [128, 1], BF16)
    costab = p.sbuf(f"la_cos_{tag}", [128, 32], F32)
    sintab = p.sbuf(f"la_sin_{tag}", [128, 32], F32)
    x_g = p.sbuf(f"la_xg_{tag}", [128, D], F32)
    y_g = p.sbuf(f"la_yg_{tag}", [128, 4 * D], BF16)
    rn1_tm = p.sbuf(f"la_rn1_{tag}", [128, D], BF16)
    rn1_fm = p.sbuf(f"la_rnf_{tag}", [128, 8 * 512], BF16)
    qa_tm = p.sbuf(f"la_qa_{tag}", [128, D], F32)
    qb_tm = p.sbuf(f"la_qb_{tag}", [128, D], F32)
    scf = p.sbuf(f"la_scf_{tag}", [128, D], F32)
    q_fm = p.sbuf(f"la_qf_{tag}", [128, 8 * 512], BF16)
    g_fm = p.sbuf(f"la_gf_{tag}", [128, 8 * 512], BF16)
    k_tm = p.sbuf(f"la_kt_{tag}", [128, 4 * D], BF16)
    v_tm = p.sbuf(f"la_vt_{tag}", [128, 4 * D], BF16)
    o_fm = p.sbuf(f"la_of_{tag}", [128, 8 * 512], BF16)
    S_sb = p.sbuf(f"la_S_{tag}", [128, 8 * 64], BF16)
    Z_sb = p.sbuf(f"la_Z_{tag}", [128, 16], BF16)
    den_sb = p.sbuf(f"la_den_{tag}", [128, 512], F32)
    den_sc = p.sbuf(f"la_dsc_{tag}", [128, 512], F32)
    q_bf = p.sbuf(f"la_qbf_{tag}", [128, D], BF16)
    ln2_tm = p.sbuf(f"la_l2_{tag}", [128, D], BF16)
    pr = p.sbuf(f"la_pr_{tag}", [128, 8], F32)
    oh1 = p.sbuf(f"la_o1_{tag}", [128, 8], BF16)
    oh2 = p.sbuf(f"la_o2_{tag}", [128, 8], BF16)
    r1s = p.sbuf(f"la_r1_{tag}", [128, 1], F32)
    r2s = p.sbuf(f"la_r2_{tag}", [128, 1], F32)
    r3s = p.sbuf(f"la_r3_{tag}", [128, 1], F32)
    oh1f = p.sbuf(f"la_h1f_{tag}", [128, T_], BF16)
    oh2f = p.sbuf(f"la_h2f_{tag}", [128, T_], BF16)
    cumx = p.sbuf(f"la_cum_{tag}", [128, min(T_, 2048)], F32)
    cl8 = p.sbuf(f"la_cl8_{tag}", [128, 1], F32)
    ones8 = p.sbuf(f"la_on8_{tag}", [128, 1], F32)
    zer8 = p.sbuf(f"la_zr8_{tag}", [128, 1], F32)
    ps_S = p.psum(f"la_pS_{tag}", [128, 8 * 64], F32)
    ps_Z = p.psum(f"la_pZ_{tag}", [128, 8], F32)
    ps_pj = p.psum(f"la_pj_{tag}", [128, 1024], F32)
    ps_num = p.psum(f"la_pn_{tag}", [128, 512], F32)
    ps_db = p.psum(f"la_pb_{tag}", [128, 512], F32)
    ps_dn = p.psum(f"la_pd_{tag}", [128, 512], F32)
    ps_tr = p.psum(f"la_ptr_{tag}", [128, 1024], BF16)

    import ml_dtypes as _md
    half = 32
    inv = 1.0 / 10000.0 ** (np.arange(half, dtype=np.float64) / half)
    ang = np.arange(T_, dtype=np.float64)[:, None] * inv[None, :]
    cos_np = np.cos(ang).astype(np.float32)
    sin_np = np.sin(ang).astype(np.float32)
    cos_hbm = p.const(cos_np, f"cos_{tag}")
    sin_hbm = p.const(sin_np, f"sin_{tag}")
    e2_np = np.zeros((2, 128), dtype=np.float32)
    e2_np[0, 0:64] = 1.0
    e2_np[1, 64:128] = 1.0

    p.start_after = p.frontier()
    tw = []
    for nm in ("wq", "wk", "wv", "wg", "wo"):
        tw.append(p.dma("qp", wsb[nm][:].rearrange("p (c i) -> p c i", c=8),
                        w[nm].rearrange("(c p) i -> p c i", p=128)))
    tw.append(p.dma("qs", rw_sb[:].rearrange("p (c i) -> p c i", c=8),
                    w["rw"].rearrange("(c p) i -> p c i", p=128)))
    tw.append(p.dma("qs", e2c[0:2, :],
                    p.const(e2_np, f"e2_{tag}")))
    onesb = p.sbuf(f"la_ob_{tag}", [1, 128], BF16)
    t_ob = p.op("v", "memset", onesb[:], 1.0)
    t_o64 = p.op("v", "memset", ones64[:], 1.0)
    t_o8 = p.op("v", "memset", ones8[:], 1.0)
    row_bf = p.sbuf(f"la_row_{tag}", [1, D], BF16)
    for (nm, src, width, dstb) in (
            ("bq", w["bq"], D, bb["bq"]), ("bk", w["bk"], D, bb["bk"]),
            ("bv", w["bv"], D, bb["bv"]), ("bo", w["bo"], D, bb["bo"]),
            ("s1", w["s1"], D, s1_b), ("s2", w["s2"], D, s2_b),
            ("rb", w["rb"], 8, rb_b)):
        tl = p.dma("qp", row_bf[0:1, 0:width], src[None, :], after=tw)
        mm = None
        for n0 in range(0, width, 512):
            wd = min(512, width - n0)
            mm = p.op("t", "matmul", ps_pj[:, n0:n0 + wd], onesb[0:1, :],
                      row_bf[0:1, n0:n0 + wd], start=True, stop=True,
                      after=[tl, t_ob])
        tw.append(p.op("v", "tensor_copy", dstb[:, 0:width] if width == D
                       else dstb[:], ps_pj[:, 0:width], after=[mm]))
    tw.append(p.dma("qs", bg_fm[:].rearrange("p (c one) -> p c one", one=1),
                    w["bg"].rearrange("(c p one) -> p c one", p=128, one=1),
                    allow_slow_non_contiguous=True))
    tw.append(p.op("p", "iota", iota8[:], pattern=[[1, 8]], base=0,
                   channel_multiplier=0,
                   allow_small_or_imprecise_dtypes=True))
    t_z = p.op("v", "memset", Z_sb[:], 0.0)
    t_s0 = p.op("v", "memset", S_sb[:], 0.0)
    perm_np = np.zeros((2, 128, 64), np.float32)
    perm_np[0, 0:64] = np.eye(64)
    perm_np[1, 64:128] = np.eye(64)
    perm_sb = p.sbuf(f"dn_perm_{tag}", [128, 128], F32)
    bde64 = p.sbuf(f"dn_b64_{tag}", [64, 8], F32)
    tpm = [p.dma("qs", perm_sb[:, 0:64], p.const(perm_np[0], f"pm0_{tag}")),
           p.dma("qs", perm_sb[:, 64:128], p.const(perm_np[1], f"pm1_{tag}"))]
    p.start_after = p.frontier()

    scan_chunk = 0
    sS_prev, den_prev = [], []
    for g in range(T_ // 512):
        prj = {}
        rn_toks = []
        ln_prev = []
        pj_prev = []
        br_prev = []
        for s4 in range(4):
            it = g * 4 + s4
            tl = p.dma("qs", x_g[:], x_in[ts(it, 128), :], after=ln_prev)
            t_ln = emit_layernorm_tile(p, x_g[:],
                                       y_g[:, ts(s4, 1024)], D, after=[tl],
                                       stats=lnb)
            t_ms = p.op("v", "tensor_tensor", r1s[:], lnb["mv"][:, 0:1],
                        lnb["mv"][:, 0:1], ALU.mult, after=[t_ln])
            t_m2 = p.op("v", "tensor_tensor", r1s[:], r1s[:],
                        lnb["mv"][:, 1:2], ALU.add, after=[t_ms])
            t_sq = p.op("a", "activation", r2s[:], r1s[:], AF.Sqrt,
                        after=[t_m2])
            t_rq = p.op("v", "reciprocal", r3s[:], r2s[:], after=[t_sq])
            t_r1 = p.op("v", "tensor_scalar_mul", qa_tm[:],
                        y_g[:, ts(s4, 1024)], r3s[:], after=[t_rq, t_ln])
            t_rn = p.op("v", "tensor_tensor", rn1_tm[:], qa_tm[:], s1_b[:],
                        ALU.mult, after=[t_r1] + tw)
            ln_prev = [t_ln, t_rn]
            trt = None
            for c in range(8):
                trt = p.op("t", "transpose", ps_tr[:, ts(c, 128)],
                           rn1_tm[:, ts(c, 128)], identb[:], after=[t_rn])
            rnf3 = bass.AP(rn1_fm[:].tensor, rn1_fm[:].offset + s4 * 128,
                           [list(rn1_fm[:].ap[0]), [512, 8], [1, 128]])
            rn_toks.append(p.op("v", "tensor_copy", rnf3,
                                ps_tr[:, 0:1024].rearrange(
                                    "p (c q) -> p c q", c=8), after=[trt]))
            tc = p.dma("qs", costab[:], cos_hbm[ts(it, 128), :],
                       after=br_prev)
            tsn = p.dma("qs", sintab[:], sin_hbm[ts(it, 128), :],
                        after=br_prev)
            for (nm, dst) in (("wq", None), ("wk", None), ("wv", None)):
                mm = None
                for n2 in range(2):
                    for c in range(8):
                        fn = p.op if (c == 7 and n2 == 1) else p.raw
                        mm = fn("t", "matmul", ps_pj[:, ts(n2, 512)],
                                rn1_fm[:, c * 512 + s4 * 128:
                                       c * 512 + s4 * 128 + 128],
                                wsb[nm][:, c * D + n2 * 512:
                                        c * D + n2 * 512 + 512],
                                start=(c == 0), stop=(c == 7),
                                after=rn_toks[-1:] + tw + rn_toks + pj_prev)
                if nm == "wv":
                    prj[("v", s4)] = p.op(
                        "v", "tensor_tensor", v_tm[:, ts(s4, 1024)],
                        ps_pj[:, 0:1024], bb["bv"][:], ALU.add, after=[mm])
                    pj_prev = [prj[("v", s4)]]
                    continue
                t_b = p.op("v", "tensor_tensor", qa_tm[:], ps_pj[:, 0:1024],
                           bb[{"wq": "bq", "wk": "bk"}[nm]][:], ALU.add,
                           after=[mm] + br_prev)
                pj_prev = [t_b]
                cosb = bass.AP(costab[:].tensor, costab[:].offset,
                               [list(costab[:].ap[0]), [0, 16], [1, 32]])
                sinb = bass.AP(sintab[:].tensor, sintab[:].offset,
                               [list(sintab[:].ap[0]), [0, 16], [1, 32]])
                def hv(buf, off):
                    a = buf[:]
                    return bass.AP(a.tensor, a.offset + off,
                                   [list(a.ap[0]), [64, 16], [1, 32]])
                x1, x2 = hv(qa_tm, 0), hv(qa_tm, 32)
                y1v, y2v = hv(qb_tm, 0), hv(qb_tm, 32)
                s1v, s2v = hv(scf, 0), hv(scf, 32)
                r_a = p.op("v", "tensor_tensor", y1v, x1, cosb, ALU.mult,
                           after=[t_b, tc, tsn])
                r_b = p.op("v", "tensor_tensor", s1v, x2, sinb, ALU.mult,
                           after=[t_b, tc, tsn])
                r_c = p.op("v", "tensor_tensor", y1v, y1v, s1v,
                           ALU.subtract, after=[r_a, r_b])
                r_d = p.op("v", "tensor_tensor", y2v, x1, sinb, ALU.mult,
                           after=[t_b, r_c])
                r_e = p.op("v", "tensor_tensor", s2v, x2, cosb, ALU.mult,
                           after=[t_b, r_c])
                r_f = p.op("v", "tensor_tensor", y2v, y2v, s2v, ALU.add,
                           after=[r_d, r_e])
                f_a = p.op("v", "tensor_scalar_min", scf[:], qb_tm[:], 0.0,
                           after=[r_f, r_c])
                f_b = p.op("a", "activation", scf[:], scf[:], AF.Exp,
                           after=[f_a])
                f_c = p.op("v", "tensor_scalar_max", qb_tm[:], qb_tm[:],
                           0.0, after=[f_b, r_f, r_c])
                if nm == "wk":
                    prj[("k", s4)] = p.op("v", "tensor_tensor",
                                          k_tm[:, ts(s4, 1024)], qb_tm[:],
                                          scf[:], ALU.add, after=[f_c, f_b])
                    br_prev = [prj[("k", s4)]]
                else:
                    t_phi = p.op("v", "tensor_tensor", q_bf[:], qb_tm[:],
                                 scf[:], ALU.add, after=[f_c, f_b])
                    trt = None
                    for c in range(8):
                        trt = p.op("t", "transpose", ps_tr[:, ts(c, 128)],
                                   q_bf[:, ts(c, 128)], identb[:],
                                   after=[t_phi])
                    qf3 = bass.AP(q_fm[:].tensor, q_fm[:].offset + s4 * 128,
                                  [list(q_fm[:].ap[0]), [512, 8], [1, 128]])
                    prj[("q", s4)] = p.op("v", "tensor_copy", qf3,
                                          ps_tr[:, 0:1024].rearrange(
                                              "p (c q) -> p c q", c=8),
                                          after=[trt])
                    br_prev = [prj[("q", s4)], t_phi]
        g_toks = []
        for ib in range(4):
            mm = None
            for icl in range(2):
                ic = ib * 2 + icl
                for c in range(8):
                    fn = p.op if (c == 7 and icl == 1) else p.raw
                    mm = fn("t", "matmul", ps_pj[:, ts(icl, 512)],
                            wsb["wg"][:, c * D + ic * 128:
                                      c * D + ic * 128 + 128],
                            rn1_fm[:, ts(c, 512)],
                            start=(c == 0), stop=(c == 7),
                            after=rn_toks + g_toks + pj_prev)
            for icl in range(2):
                ic = ib * 2 + icl
                g_toks.append(
                    p.op("a", "activation", g_fm[:, ts(ic, 512)],
                         ps_pj[:, ts(icl, 512)], AF.Sigmoid,
                         bias=bg_fm[:, ic:ic + 1], after=[mm]))
        pj_prev = g_toks[-2:]
        for cc in range(8):
            s4, hh = cc // 2, cc % 2
            b0 = 64 * hh
            gchunk = scan_chunk
            scan_chunk += 1
            tok_S = None
            for pp in range(8):
                for hb in range(2):
                    hd = 2 * pp + hb
                    lhs = k_tm[b0:b0 + 64,
                               s4 * 1024 + hd * 64: s4 * 1024 + hd * 64 + 64]
                    rhs = v_tm[b0:b0 + 64,
                               s4 * 1024 + hd * 64: s4 * 1024 + hd * 64 + 64]
                    p.raw("t", "matmul",
                          ps_S[64 * hb:64 * hb + 64, ts(pp, 64)],
                          lhs, rhs, start=(gchunk == 0),
                          stop=(gchunk == T_ // 64 - 1),
                          after=[prj[("k", s4)], prj[("v", s4)]]
                          + sS_prev)
                    fn = p.op if (pp == 7 and hb == 1) else p.raw
                    tok_S = fn("t", "matmul",
                               ps_Z[64 * hb:64 * hb + 64, pp:pp + 1],
                               lhs, ones64[b0:b0 + 64, :],
                               start=(gchunk == 0),
                               stop=(gchunk == T_ // 64 - 1))
            ev_toks = []
            for pp in range(8):
                for hb in range(2):
                    hd = 2 * pp + hb
                    ev_toks.append(p.op(
                        "v", "tensor_copy",
                        S_sb[64 * hb:64 * hb + 64, ts(pp, 64)],
                        ps_S[64 * hb:64 * hb + 64, ts(pp, 64)],
                        after=[tok_S]))
                    ev_toks.append(p.op(
                        "v", "tensor_copy",
                        Z_sb[64 * hb:64 * hb + 64, hd:hd + 1],
                        ps_Z[64 * hb:64 * hb + 64, pp:pp + 1],
                        after=[tok_S]))
            sS_prev = ev_toks
            mmN = None
            for pp in range(8):
                for hb in range(2):
                    mmN = p.raw("t", "matmul",
                                ps_num[64 * hb:64 * hb + 64, ts(pp, 64)],
                                S_sb[64 * hb:64 * hb + 64, ts(pp, 64)],
                                q_fm[64 * hb:64 * hb + 64,
                                     pp * 512 + cc * 64: pp * 512 + cc * 64
                                     + 64],
                                start=True, stop=True,
                                after=ev_toks + [prj[("q", cc // 2)]]
                                + den_prev)
                mmN = p.raw("t", "matmul", ps_dn[0:2, ts(pp, 64)],
                            Z_sb[:, 2 * pp:2 * pp + 2],
                            q_fm[:, pp * 512 + cc * 64:
                                 pp * 512 + cc * 64 + 64],
                            start=True, stop=True)
            t_nd = p.op("t", "matmul", ps_dn[0:2, 7 * 64:7 * 64 + 64],
                        Z_sb[:, 14:16],
                        q_fm[:, 7 * 512 + cc * 64: 7 * 512 + cc * 64 + 64],
                        start=True, stop=True)
            t_de = p.op("v", "tensor_scalar", den_sb[0:2, :],
                        ps_dn[0:2, 0:512], 1e-6, None, ALU.add,
                        after=[t_nd])
            t_dr = p.op("v", "reciprocal", den_sc[0:2, :],
                        den_sb[0:2, :], after=[t_de])
            mmB = None
            for pp in range(8):
                fn = p.op if pp == 7 else p.raw
                mmB = fn("t", "matmul", ps_db[:, ts(pp, 64)], e2c[0:2, :],
                         den_sc[0:2, ts(pp, 64)], start=True, stop=True,
                         after=[t_dr])
            gsl = bass.AP(g_fm[:].tensor, g_fm[:].offset + cc * 64,
                          [list(g_fm[:].ap[0]), [512, 8], [1, 64]])
            osl = bass.AP(o_fm[:].tensor, o_fm[:].offset + cc * 64,
                          [list(o_fm[:].ap[0]), [512, 8], [1, 64]])
            if _os.environ.get("S3_DBG") and cc == 6:
                dbg1 = p.dram(f"dbg_den_{tag}", [4, 512], F32,
                              kind="ExternalOutput")
                dbg2 = p.dram(f"dbg_num_{tag}", [128, 512], F32,
                              kind="ExternalOutput")
                dbg3 = p.dram(f"dbg_S_{tag}", [128, 512], F32,
                              kind="ExternalOutput")
                p.dma("qs", dbg1[0:2, :], den_sb[0:2, :], after=[t_dr])
                p.dma("qs", dbg1[2:4, :], den_sc[0:2, :], after=[t_dr])
                tnum = p.op("v", "tensor_copy", qa_tm[:, 0:512],
                            ps_num[:, 0:512], after=[t_nd])
                p.dma("qs", dbg2, qa_tm[:, 0:512], after=[tnum])
                tS = p.op("v", "tensor_copy", qb_tm[:, 0:512],
                          S_sb[:, 0:512], after=ev_toks)
                p.dma("qs", dbg3, qb_tm[:, 0:512], after=[tS])
            t_g1 = p.op("v", "tensor_tensor", den_sb[:, 0:512],
                        ps_num[:, 0:512], gsl, ALU.mult,
                        after=[t_nd, t_dr] + g_toks)
            den_prev = [p.op("v", "tensor_tensor", osl, den_sb[:, 0:512],
                             ps_db[:, 0:512], ALU.mult, after=[t_g1, mmB]),
                        t_g1]
        p.start_after = p.frontier()
        d_prev = []
        for s4 in range(4):
            it = g * 4 + s4
            mm = None
            for n2 in range(2):
                for c in range(8):
                    fn = p.op if (c == 7 and n2 == 1) else p.raw
                    mm = fn("t", "matmul", ps_pj[:, ts(n2, 512)],
                            o_fm[:, c * 512 + s4 * 128:
                                 c * 512 + s4 * 128 + 128],
                            wsb["wo"][:, c * D + n2 * 512:
                                      c * D + n2 * 512 + 512],
                            start=(c == 0), stop=(c == 7),
                            after=den_prev + rn_toks + d_prev + pj_prev)
            tlx = p.dma("qs", x_g[:], x_in[ts(it, 128), :], after=d_prev)
            t_y1 = p.op("v", "tensor_tensor", qa_tm[:], ps_pj[:, 0:1024],
                        bb["bo"][:], ALU.add, after=[mm])
            pj_prev = [t_y1]
            t_y2 = p.op("v", "tensor_tensor", y_g[:, ts(s4, 1024)],
                        y_g[:, ts(s4, 1024)], qa_tm[:], ALU.add,
                        after=[t_y1])
            t_b3 = p.op("v", "tensor_tensor", x_g[:], x_g[:],
                        y_g[:, ts(s4, 1024)], ALU.add, after=[t_y2, tlx])
            t_b3d = p.dma("qs", base3_out[ts(it, 128), :], x_g[:],
                          after=[t_b3])
            t_s = p.op("a", "activation", qa_tm[:], y_g[:, ts(s4, 1024)],
                       AF.Square, after=[t_y2, t_b3] + d_prev)
            t_ss = p.op("v", "reduce_sum", r1s[:], qa_tm[:], AX.X,
                        after=[t_s])
            t_mn = p.op("v", "tensor_scalar_mul", r1s[:], r1s[:],
                        1.0 / 1024.0, after=[t_ss])
            t_sq2 = p.op("a", "activation", r2s[:], r1s[:], AF.Sqrt,
                         after=[t_mn])
            t_rq2 = p.op("v", "reciprocal", r3s[:], r2s[:], after=[t_sq2])
            t_l2a = p.op("v", "tensor_scalar_mul", qa_tm[:],
                         y_g[:, ts(s4, 1024)], r3s[:], after=[t_rq2])
            t_l2f = p.op("v", "tensor_tensor", scf[:], qa_tm[:], s2_b[:],
                         ALU.mult, after=[t_l2a])
            t_l2 = p.op("v", "tensor_copy", ln2_tm[:], scf[:],
                        after=[t_l2f])
            p.dma("qp", ln2_out[ts(it, 128), :], ln2_tm[:], after=[t_l2])
            trt = None
            for c in range(8):
                trt = p.op("t", "transpose", ps_pj[:, ts(c, 128)],
                           scf[:, ts(c, 128)], identf[:],
                           after=[t_l2f, t_l2] + d_prev)
            t_lf = p.op("v", "tensor_copy", qb_tm[:], ps_pj[:, 0:1024],
                        after=[trt])
            mmr = None
            for c in range(8):
                fn = p.op if c == 7 else p.raw
                mmr = fn("t", "matmul", ps_num[:, 0:8],
                         qb_tm[:, ts(c, 128)], rw_sb[:, c * 8:c * 8 + 8],
                         start=(c == 0), stop=(c == 7), after=[t_lf])
            t_lg = p.op("v", "tensor_tensor", pr[:], ps_num[:, 0:8],
                        rb_b[:, 0:8], ALU.add, after=[mmr])
            t_m = p.op("v", "reduce_max", r1s[:], pr[:], AX.X, after=[t_lg])
            t_nm = p.op("v", "tensor_scalar_mul", r1s[:], r1s[:], -1.0,
                        after=[t_m])
            t_e = p.op("a", "activation", pr[:], pr[:], AF.Exp,
                       bias=r1s[:], after=[t_nm])
            t_sm = p.op("v", "reduce_sum", r2s[:], pr[:], AX.X, after=[t_e])
            t_ri = p.op("v", "reciprocal", r3s[:], r2s[:], after=[t_sm])
            t_p = p.op("v", "tensor_scalar_mul", pr[:], pr[:], r3s[:],
                       after=[t_ri, t_e])
            t_m1 = p.op("v", "reduce_max", r1s[:], pr[:], AX.X, after=[t_p])
            t_q1 = p.op("v", "tensor_scalar", oh1[:], pr[:], r1s[:], None,
                        ALU.is_ge, after=[t_m1, t_p])
            t_mk = p.op("v", "scalar_tensor_tensor", qb_tm[:, 0:8],
                        oh1[:], -2.0, pr[:], ALU.mult, ALU.add,
                        after=[t_q1, t_p])
            t_m2 = p.op("v", "reduce_max", r2s[:], qb_tm[:, 0:8], AX.X,
                        after=[t_mk])
            t_q2 = p.op("v", "tensor_scalar", oh2[:], pr[:], r2s[:], None,
                        ALU.is_ge, after=[t_m2, t_p])
            t_q2b = p.op("v", "tensor_tensor", oh2[:], oh2[:], oh1[:],
                         ALU.subtract, after=[t_q2, t_q1])
            t_i1 = p.op("v", "tensor_tensor", qb_tm[:, 0:8], oh1[:],
                        iota8[:], ALU.mult, after=[t_q1, t_m2] + tw)
            t_i1r = p.op("v", "reduce_sum", qb_tm[:, 8:9], qb_tm[:, 0:8],
                         AX.X, after=[t_i1])
            p.dma("qs", idx1_o[ts(it, 128), :], qb_tm[:, 8:9], after=[t_i1r])
            t_i2 = p.op("v", "tensor_tensor", qb_tm[:, 0:8], oh2[:],
                        iota8[:], ALU.mult, after=[t_q2b] + tw)
            t_i2r = p.op("v", "reduce_sum", qb_tm[:, 9:10], qb_tm[:, 0:8],
                         AX.X, after=[t_i2])
            p.dma("qs", idx2_o[ts(it, 128), :], qb_tm[:, 9:10], after=[t_i2r])
            t_sg = p.op("v", "tensor_tensor", qb_tm[:, 10:11], r1s[:],
                        r2s[:], ALU.add, after=[t_m2, t_m1])
            t_sr = p.op("v", "reciprocal", qb_tm[:, 11:12],
                        qb_tm[:, 10:11], after=[t_sg])
            t_g1 = p.op("v", "tensor_tensor", qb_tm[:, 12:13], r1s[:],
                        qb_tm[:, 11:12], ALU.mult, after=[t_sr])
            t_g2 = p.op("v", "tensor_tensor", qb_tm[:, 13:14], r2s[:],
                        qb_tm[:, 11:12], ALU.mult, after=[t_sr])
            p.dma("qs", g1_o[ts(it, 128), :], qb_tm[:, 12:13], after=[t_g1])
            p.dma("qs", g2_o[ts(it, 128), :], qb_tm[:, 13:14], after=[t_g2])
            tr1 = p.op("t", "transpose", ps_tr[0:8, 0:128], oh1[:],
                       identb[:], after=[t_q1, t_i1, t_lf])
            tr2 = p.op("t", "transpose", ps_tr[0:8, 128:256], oh2[:],
                       identb[:], after=[t_q2b, t_i2])
            t_c1 = p.op("v", "tensor_copy", oh1f[0:8, ts(it, 128)],
                        ps_tr[0:8, 0:128], after=[tr2])
            t_c2 = p.op("v", "tensor_copy", oh2f[0:8, ts(it, 128)],
                        ps_tr[0:8, 128:256], after=[tr2])
            d_prev = [t_c1, t_c2, t_b3d, t_g1, t_g2, t_i1r, t_i2r, t_q2b,
                      t_e]
        p.start_after = p.frontier()
    HW2 = min(T_, 2048)
    t_zz = p.op("v", "memset", zer8[:], 0.0)
    z8b = bass.AP(zer8[:].tensor, zer8[:].offset,
                  [[zer8[:].ap[0][0], 8], [0, HW2]])
    t_cl = p.op("v", "memset", cl8[:], 0.0)
    mm = None
    for h2 in range(T_ // HW2):
        c0 = h2 * HW2
        t_A = p.op("v", "tensor_tensor", cumx[0:8, :],
                   oh1f[0:8, c0:c0 + HW2], oh2f[0:8, c0:c0 + HW2],
                   ALU.add, after=([mm] if mm else []) + [t_cl])
        t_cs = p.op("v", "tensor_tensor_scan", cumx[0:8, :], z8b,
                    cumx[0:8, :], cl8[0:8, :], ALU.add, ALU.add,
                    after=[t_A, t_zz])
        t_cl = p.op("v", "tensor_copy", cl8[0:8, :],
                    cumx[0:8, HW2 - 1:HW2], after=[t_cs])
        if h2 == T_ // HW2 - 1:
            t_cl = p.dma("qs", tot_o, cumx[0:8, HW2 - 1:HW2],
                         after=[t_cs, t_cl])
        t_e1 = p.op("v", "tensor_tensor", cumx[0:8, :], cumx[0:8, :],
                    oh1f[0:8, c0:c0 + HW2], ALU.subtract,
                    after=[t_cs, t_cl])
        t_ex = p.op("v", "tensor_tensor", cumx[0:8, :], cumx[0:8, :],
                    oh2f[0:8, c0:c0 + HW2], ALU.subtract, after=[t_e1])
        for (ohf, dst) in ((oh1f, rk1_o), (oh2f, rk2_o)):
            for gg in range(HW2 // 512):
                t_t = p.op("v", "tensor_tensor", den_sb[0:8, :],
                           cumx[0:8, ts(gg, 512)],
                           ohf[0:8, c0 + gg * 512:c0 + gg * 512 + 512],
                           ALU.mult, after=[t_ex] + ([mm] if mm else []))
                mm = p.op("t", "matmul", ps_num[0:1, 0:512], ones8[0:8, :],
                          den_sb[0:8, :], start=True, stop=True,
                          after=[t_t])
                mm = p.op("v", "tensor_copy", den_sc[0:1, 0:512],
                          ps_num[0:1, 0:512], after=[mm])
                p.dma("qs", dst[c0 + gg * 512:c0 + gg * 512 + 512,
                                :].rearrange("t one -> one t"),
                      den_sc[0:1, 0:512], after=[mm])
    p.start_after = p.frontier()
    return p.frontier()


def emit_deltanet(p, x_in, x_out, ww_ap, wr_ap, wb_ap, decay_ap, lnb,
                  identb, identf, tag):
    T_ = x_in.shape[0]
    C = 64
    wwT = p.sbuf(f"dn_wwT_{tag}", [128, 8 * D], BF16)
    wrT = p.sbuf(f"dn_wrT_{tag}", [128, 8 * D], BF16)
    wbT = p.sbuf(f"dn_wbT_{tag}", [128, 8 * 8], BF16)
    wld = p.sbuf(f"dn_wld_{tag}", [128, D], F32)
    wlb = p.sbuf(f"dn_wlb_{tag}", [128, D], BF16)
    x_tm = p.sbuf(f"dn_x_{tag}", [128, D], F32)
    ln_tm = p.sbuf(f"dn_ln_{tag}", [128, D], F32)
    lnb16 = p.sbuf(f"dn_lnb_{tag}", [128, D], BF16)
    lnf_sub = p.sbuf(f"dn_lnf_{tag}", [128, D], BF16)
    sq_tm = p.sbuf(f"dn_sq_{tag}", [128, D], F32)
    ss8 = p.sbuf(f"dn_ss_{tag}", [128, 8], F32)
    rk_tm = p.sbuf(f"dn_rk_{tag}", [128, D], BF16)
    beta_tm = p.sbuf(f"dn_be_{tag}", [128, 8], F32)
    bde_tm = p.sbuf(f"dn_bde_{tag}", [128, 4 * 8], F32)
    rfA_tm = p.sbuf(f"dn_rfA_{tag}", [128, 8], F32)
    rk_fm = p.sbuf(f"dn_rkf_{tag}", [128, 8 * 513], BF16)
    v_tm = p.sbuf(f"dn_vt_{tag}", [128, 4 * D], BF16)
    o_fm = p.sbuf(f"dn_of_{tag}", [128, 8 * 512], BF16)
    lg_b = p.sbuf(f"dn_lgb_{tag}", [128, 8], F32)
    gp1 = p.sbuf(f"dn_gp1_{tag}", [128, 8], F32)
    de_tm = p.sbuf(f"dn_de_{tag}", [128, 8], F32)
    dw_tm = p.sbuf(f"dn_dw_{tag}", [128, 8], F32)
    rfT_tm = p.sbuf(f"dn_rfT_{tag}", [128, 8], F32)
    ct8 = p.sbuf(f"dn_ct_{tag}", [128, 8], F32)
    cmA = p.sbuf(f"dn_cmA_{tag}", [64, 8 * 64], F32)
    cmT = p.sbuf(f"dn_cmT_{tag}", [64, 8 * 64], F32)
    eyeT = p.sbuf(f"dn_eye_{tag}", [64, 8 * 64], BF16)
    debc = p.sbuf(f"dn_debc_{tag}", [128, 8 * 64], F32)
    A_sb = p.sbuf(f"dn_A_{tag}", [64, 8 * 64], BF16)
    B_sb = p.sbuf(f"dn_B_{tag}", [64, 8 * 64], BF16)
    Bt_sb = [p.sbuf(f"dn_Bt{k}_{tag}", [64, 8 * 64], BF16) for k in range(6)]
    T_sb = p.sbuf(f"dn_T_{tag}", [64, 8 * 64], BF16)
    Tt_sb = p.sbuf(f"dn_Tt_{tag}", [64, 8 * 64], BF16)
    X_sb = p.sbuf(f"dn_X_{tag}", [64, 8 * 256], BF16)
    Xp_sb = p.sbuf(f"dn_Xp_{tag}", [64, 8 * 256], BF16)
    wkdw = p.sbuf(f"dn_wkdw_{tag}", [64, 8 * 128], F32)
    wkcf = p.sbuf(f"dn_wkcf_{tag}", [128, 8 * 64], F32)
    rkde = p.sbuf(f"dn_rkde_{tag}", [128, 8 * 64], F32)
    vnew = p.sbuf(f"dn_vnew_{tag}", [64, 8 * 128], F32)
    attnT = p.sbuf(f"dn_at_{tag}", [64, 8 * 64], F32)
    S_sb = p.sbuf(f"dn_S_{tag}", [128, 8 * 128], F32)
    row1 = p.sbuf(f"dn_row_{tag}", [1, 128], F32)
    P1 = p.psum(f"dn_p1_{tag}", [64, 1024], F32)
    P2 = p.psum(f"dn_p2_{tag}", [64, 1024], F32)
    P3 = p.psum(f"dn_p3_{tag}", [128, 1024], F32)
    ps_tr = p.psum(f"dn_tr_{tag}", [128, 1024], BF16)

    SUBS = int(_os.environ.get("S1_SUBS", "9"))
    p.start_after = p.frontier()
    tw = []
    prev = []
    for (w_ap, dstT) in (((ww_ap, wwT), (wr_ap, wrT)) if SUBS >= 1 else ()):
        for r in range(8):
            tl = p.dma("qp", lnb16[:], w_ap[ts(r, 128), :], after=prev)
            trt = None
            for c in range(8):
                trt = p.op("t", "transpose", ps_tr[:, ts(c, 128)],
                           lnb16[:, ts(c, 128)], identb[:],
                           after=[tl] + prev)
            prev = [p.op("v", "tensor_copy",
                         bass.AP(dstT[:].tensor,
                                 dstT[:].offset + r * 128,
                                 [list(dstT[:].ap[0]), [D, 8], [1, 128]]),
                         ps_tr[:, 0:1024].rearrange("p (c q) -> p c q", c=8),
                         after=[trt])]
        tw += prev
    if SUBS < 2:
        tlb = p.op("v", "memset", wbT[:], 0.01)
        tw.append(tlb)
    tlb = p.dma("qp", wlb[0:8, :], wb_ap, after=[]) if SUBS >= 2 else tlb
    prevb = []
    for c in range(8 if SUBS >= 2 else 0):
        trt = p.op("t", "transpose", ps_tr[:, ts(c, 128)][0:128, 0:8],
                   wlb[0:8, ts(c, 128)], identb[0:8, 0:8],
                   after=[tlb] + prevb)
        prevb = [p.op("v", "tensor_copy", wbT[:, c * 8:c * 8 + 8],
                      ps_tr[:, ts(c, 128)][:, 0:8], after=[trt])]
    tw += prevb
    if SUBS < 1:
        tw.append(p.op("v", "memset", wwT[:], 0.01))
        tw.append(p.op("v", "memset", wrT[:], 0.01))
    if SUBS < 3:
        for bufz in (lg_b, gp1, dw_tm, rfT_tm, ct8):
            p.op("v", "memset", bufz[:], 0.01)
        for bufz in (cmA, cmT):
            p.op("v", "memset", bufz[:], 0.01)
        p.op("v", "memset", debc[:], 0.01)
    tdec = p.dma("qs", wld[0:1, 0:8], decay_ap[None, :], after=[])
    cm_toks, cm2_toks, db_toks = [], [], []
    t_cte = t_dwe = t_rfe = t_b64tok = None
    if SUBS >= 3:
        t_sg = p.op("a", "activation", wld[0:1, 0:8], wld[0:1, 0:8],
                    AF.Sigmoid, after=[tdec])
    else:
        t_cte = t_dwe = t_rfe = p.op("v", "memset", bde64[:] if False else
                                     lg_b[:, 0:1], 0.01)
        t_lgb = t_g1e = t_cte
    t_s0 = p.op("v", "memset", S_sb[:], 0.0)
    perm_np = np.zeros((2, 128, 64), np.float32)
    perm_np[0, 0:64] = np.eye(64)
    perm_np[1, 64:128] = np.eye(64)
    perm_sb = p.sbuf(f"dn_perm_{tag}", [128, 128], F32)
    bde64 = p.sbuf(f"dn_b64_{tag}", [64, 8], F32)
    tpm = [p.dma("qs", perm_sb[:, 0:64], p.const(perm_np[0], f"pm0_{tag}")),
           p.dma("qs", perm_sb[:, 64:128], p.const(perm_np[1], f"pm1_{tag}"))]
    p.start_after = p.frontier()

    scan_chunk = 0
    sS_prev, den_prev = [], []
    for g in range(T_ // 512):
        prj = {}
        rn_toks = []
        ln_prev = []
        pj_prev = []
        br_prev = []
        for s4 in range(4):
            it = g * 4 + s4
            tl = p.dma("qs", x_g[:], x_in[ts(it, 128), :], after=ln_prev)
            t_ln = emit_layernorm_tile(p, x_g[:],
                                       y_g[:, ts(s4, 1024)], D, after=[tl],
                                       stats=lnb)
            t_ms = p.op("v", "tensor_tensor", r1s[:], lnb["mv"][:, 0:1],
                        lnb["mv"][:, 0:1], ALU.mult, after=[t_ln])
            t_m2 = p.op("v", "tensor_tensor", r1s[:], r1s[:],
                        lnb["mv"][:, 1:2], ALU.add, after=[t_ms])
            t_sq = p.op("a", "activation", r2s[:], r1s[:], AF.Sqrt,
                        after=[t_m2])
            t_rq = p.op("v", "reciprocal", r3s[:], r2s[:], after=[t_sq])
            t_r1 = p.op("v", "tensor_scalar_mul", qa_tm[:],
                        y_g[:, ts(s4, 1024)], r3s[:], after=[t_rq, t_ln])
            t_rn = p.op("v", "tensor_tensor", rn1_tm[:], qa_tm[:], s1_b[:],
                        ALU.mult, after=[t_r1] + tw)
            ln_prev = [t_ln, t_rn]
            trt = None
            for c in range(8):
                trt = p.op("t", "transpose", ps_tr[:, ts(c, 128)],
                           rn1_tm[:, ts(c, 128)], identb[:], after=[t_rn])
            rnf3 = bass.AP(rn1_fm[:].tensor, rn1_fm[:].offset + s4 * 128,
                           [list(rn1_fm[:].ap[0]), [512, 8], [1, 128]])
            rn_toks.append(p.op("v", "tensor_copy", rnf3,
                                ps_tr[:, 0:1024].rearrange(
                                    "p (c q) -> p c q", c=8), after=[trt]))
            tc = p.dma("qs", costab[:], cos_hbm[ts(it, 128), :],
                       after=br_prev)
            tsn = p.dma("qs", sintab[:], sin_hbm[ts(it, 128), :],
                        after=br_prev)
            for (nm, dst) in (("wq", None), ("wk", None), ("wv", None)):
                mm = None
                for n2 in range(2):
                    for c in range(8):
                        fn = p.op if (c == 7 and n2 == 1) else p.raw
                        mm = fn("t", "matmul", ps_pj[:, ts(n2, 512)],
                                rn1_fm[:, c * 512 + s4 * 128:
                                       c * 512 + s4 * 128 + 128],
                                wsb[nm][:, c * D + n2 * 512:
                                        c * D + n2 * 512 + 512],
                                start=(c == 0), stop=(c == 7),
                                after=rn_toks[-1:] + tw + rn_toks + pj_prev)
                if nm == "wv":
                    prj[("v", s4)] = p.op(
                        "v", "tensor_tensor", v_tm[:, ts(s4, 1024)],
                        ps_pj[:, 0:1024], bb["bv"][:], ALU.add, after=[mm])
                    pj_prev = [prj[("v", s4)]]
                    continue
                t_b = p.op("v", "tensor_tensor", qa_tm[:], ps_pj[:, 0:1024],
                           bb[{"wq": "bq", "wk": "bk"}[nm]][:], ALU.add,
                           after=[mm] + br_prev)
                pj_prev = [t_b]
                cosb = bass.AP(costab[:].tensor, costab[:].offset,
                               [list(costab[:].ap[0]), [0, 16], [1, 32]])
                sinb = bass.AP(sintab[:].tensor, sintab[:].offset,
                               [list(sintab[:].ap[0]), [0, 16], [1, 32]])
                def hv(buf, off):
                    a = buf[:]
                    return bass.AP(a.tensor, a.offset + off,
                                   [list(a.ap[0]), [64, 16], [1, 32]])
                x1, x2 = hv(qa_tm, 0), hv(qa_tm, 32)
                y1v, y2v = hv(qb_tm, 0), hv(qb_tm, 32)
                s1v, s2v = hv(scf, 0), hv(scf, 32)
                r_a = p.op("v", "tensor_tensor", y1v, x1, cosb, ALU.mult,
                           after=[t_b, tc, tsn])
                r_b = p.op("v", "tensor_tensor", s1v, x2, sinb, ALU.mult,
                           after=[t_b, tc, tsn])
                r_c = p.op("v", "tensor_tensor", y1v, y1v, s1v,
                           ALU.subtract, after=[r_a, r_b])
                r_d = p.op("v", "tensor_tensor", y2v, x1, sinb, ALU.mult,
                           after=[t_b, r_c])
                r_e = p.op("v", "tensor_tensor", s2v, x2, cosb, ALU.mult,
                           after=[t_b, r_c])
                r_f = p.op("v", "tensor_tensor", y2v, y2v, s2v, ALU.add,
                           after=[r_d, r_e])
                f_a = p.op("v", "tensor_scalar_min", scf[:], qb_tm[:], 0.0,
                           after=[r_f, r_c])
                f_b = p.op("a", "activation", scf[:], scf[:], AF.Exp,
                           after=[f_a])
                f_c = p.op("v", "tensor_scalar_max", qb_tm[:], qb_tm[:],
                           0.0, after=[f_b, r_f, r_c])
                if nm == "wk":
                    prj[("k", s4)] = p.op("v", "tensor_tensor",
                                          k_tm[:, ts(s4, 1024)], qb_tm[:],
                                          scf[:], ALU.add, after=[f_c, f_b])
                    br_prev = [prj[("k", s4)]]
                else:
                    t_phi = p.op("v", "tensor_tensor", q_bf[:], qb_tm[:],
                                 scf[:], ALU.add, after=[f_c, f_b])
                    trt = None
                    for c in range(8):
                        trt = p.op("t", "transpose", ps_tr[:, ts(c, 128)],
                                   q_bf[:, ts(c, 128)], identb[:],
                                   after=[t_phi])
                    qf3 = bass.AP(q_fm[:].tensor, q_fm[:].offset + s4 * 128,
                                  [list(q_fm[:].ap[0]), [512, 8], [1, 128]])
                    prj[("q", s4)] = p.op("v", "tensor_copy", qf3,
                                          ps_tr[:, 0:1024].rearrange(
                                              "p (c q) -> p c q", c=8),
                                          after=[trt])
                    br_prev = [prj[("q", s4)], t_phi]
        g_toks = []
        for ib in range(4):
            mm = None
            for icl in range(2):
                ic = ib * 2 + icl
                for c in range(8):
                    fn = p.op if (c == 7 and icl == 1) else p.raw
                    mm = fn("t", "matmul", ps_pj[:, ts(icl, 512)],
                            wsb["wg"][:, c * D + ic * 128:
                                      c * D + ic * 128 + 128],
                            rn1_fm[:, ts(c, 512)],
                            start=(c == 0), stop=(c == 7),
                            after=rn_toks + g_toks + pj_prev)
            for icl in range(2):
                ic = ib * 2 + icl
                g_toks.append(
                    p.op("a", "activation", g_fm[:, ts(ic, 512)],
                         ps_pj[:, ts(icl, 512)], AF.Sigmoid,
                         bias=bg_fm[:, ic:ic + 1], after=[mm]))
        pj_prev = g_toks[-2:]
        for cc in range(8):
            s4, hh = cc // 2, cc % 2
            b0 = 64 * hh
            gchunk = scan_chunk
            scan_chunk += 1
            tok_S = None
            for pp in range(8):
                for hb in range(2):
                    hd = 2 * pp + hb
                    lhs = k_tm[b0:b0 + 64,
                               s4 * 1024 + hd * 64: s4 * 1024 + hd * 64 + 64]
                    rhs = v_tm[b0:b0 + 64,
                               s4 * 1024 + hd * 64: s4 * 1024 + hd * 64 + 64]
                    p.raw("t", "matmul",
                          ps_S[64 * hb:64 * hb + 64, ts(pp, 64)],
                          lhs, rhs, start=(gchunk == 0),
                          stop=(gchunk == T_ // 64 - 1),
                          after=[prj[("k", s4)], prj[("v", s4)]]
                          + sS_prev)
                    fn = p.op if (pp == 7 and hb == 1) else p.raw
                    tok_S = fn("t", "matmul",
                               ps_Z[64 * hb:64 * hb + 64, pp:pp + 1],
                               lhs, ones64[b0:b0 + 64, :],
                               start=(gchunk == 0),
                               stop=(gchunk == T_ // 64 - 1))
            ev_toks = []
            for pp in range(8):
                for hb in range(2):
                    hd = 2 * pp + hb
                    ev_toks.append(p.op(
                        "v", "tensor_copy",
                        S_sb[64 * hb:64 * hb + 64, ts(pp, 64)],
                        ps_S[64 * hb:64 * hb + 64, ts(pp, 64)],
                        after=[tok_S]))
                    ev_toks.append(p.op(
                        "v", "tensor_copy",
                        Z_sb[64 * hb:64 * hb + 64, hd:hd + 1],
                        ps_Z[64 * hb:64 * hb + 64, pp:pp + 1],
                        after=[tok_S]))
            sS_prev = ev_toks
            mmN = None
            for pp in range(8):
                for hb in range(2):
                    mmN = p.raw("t", "matmul",
                                ps_num[64 * hb:64 * hb + 64, ts(pp, 64)],
                                S_sb[64 * hb:64 * hb + 64, ts(pp, 64)],
                                q_fm[64 * hb:64 * hb + 64,
                                     pp * 512 + cc * 64: pp * 512 + cc * 64
                                     + 64],
                                start=True, stop=True,
                                after=ev_toks + [prj[("q", cc // 2)]]
                                + den_prev)
                mmN = p.raw("t", "matmul", ps_dn[0:2, ts(pp, 64)],
                            Z_sb[:, 2 * pp:2 * pp + 2],
                            q_fm[:, pp * 512 + cc * 64:
                                 pp * 512 + cc * 64 + 64],
                            start=True, stop=True)
            t_nd = p.op("t", "matmul", ps_dn[0:2, 7 * 64:7 * 64 + 64],
                        Z_sb[:, 14:16],
                        q_fm[:, 7 * 512 + cc * 64: 7 * 512 + cc * 64 + 64],
                        start=True, stop=True)
            t_de = p.op("v", "tensor_scalar", den_sb[0:2, :],
                        ps_dn[0:2, 0:512], 1e-6, None, ALU.add,
                        after=[t_nd])
            t_dr = p.op("v", "reciprocal", den_sc[0:2, :],
                        den_sb[0:2, :], after=[t_de])
            mmB = None
            for pp in range(8):
                fn = p.op if pp == 7 else p.raw
                mmB = fn("t", "matmul", ps_db[:, ts(pp, 64)], e2c[0:2, :],
                         den_sc[0:2, ts(pp, 64)], start=True, stop=True,
                         after=[t_dr])
            gsl = bass.AP(g_fm[:].tensor, g_fm[:].offset + cc * 64,
                          [list(g_fm[:].ap[0]), [512, 8], [1, 64]])
            osl = bass.AP(o_fm[:].tensor, o_fm[:].offset + cc * 64,
                          [list(o_fm[:].ap[0]), [512, 8], [1, 64]])
            if _os.environ.get("S3_DBG") and cc == 6:
                dbg1 = p.dram(f"dbg_den_{tag}", [4, 512], F32,
                              kind="ExternalOutput")
                dbg2 = p.dram(f"dbg_num_{tag}", [128, 512], F32,
                              kind="ExternalOutput")
                dbg3 = p.dram(f"dbg_S_{tag}", [128, 512], F32,
                              kind="ExternalOutput")
                p.dma("qs", dbg1[0:2, :], den_sb[0:2, :], after=[t_dr])
                p.dma("qs", dbg1[2:4, :], den_sc[0:2, :], after=[t_dr])
                tnum = p.op("v", "tensor_copy", qa_tm[:, 0:512],
                            ps_num[:, 0:512], after=[t_nd])
                p.dma("qs", dbg2, qa_tm[:, 0:512], after=[tnum])
                tS = p.op("v", "tensor_copy", qb_tm[:, 0:512],
                          S_sb[:, 0:512], after=ev_toks)
                p.dma("qs", dbg3, qb_tm[:, 0:512], after=[tS])
            t_g1 = p.op("v", "tensor_tensor", den_sb[:, 0:512],
                        ps_num[:, 0:512], gsl, ALU.mult,
                        after=[t_nd, t_dr] + g_toks)
            den_prev = [p.op("v", "tensor_tensor", osl, den_sb[:, 0:512],
                             ps_db[:, 0:512], ALU.mult, after=[t_g1, mmB]),
                        t_g1]
        p.start_after = p.frontier()
        d_prev = []
        for s4 in range(4):
            it = g * 4 + s4
            mm = None
            for n2 in range(2):
                for c in range(8):
                    fn = p.op if (c == 7 and n2 == 1) else p.raw
                    mm = fn("t", "matmul", ps_pj[:, ts(n2, 512)],
                            o_fm[:, c * 512 + s4 * 128:
                                 c * 512 + s4 * 128 + 128],
                            wsb["wo"][:, c * D + n2 * 512:
                                      c * D + n2 * 512 + 512],
                            start=(c == 0), stop=(c == 7),
                            after=den_prev + rn_toks + d_prev + pj_prev)
            tlx = p.dma("qs", x_g[:], x_in[ts(it, 128), :], after=d_prev)
            t_y1 = p.op("v", "tensor_tensor", qa_tm[:], ps_pj[:, 0:1024],
                        bb["bo"][:], ALU.add, after=[mm])
            pj_prev = [t_y1]
            t_y2 = p.op("v", "tensor_tensor", y_g[:, ts(s4, 1024)],
                        y_g[:, ts(s4, 1024)], qa_tm[:], ALU.add,
                        after=[t_y1])
            t_b3 = p.op("v", "tensor_tensor", x_g[:], x_g[:],
                        y_g[:, ts(s4, 1024)], ALU.add, after=[t_y2, tlx])
            t_b3d = p.dma("qs", base3_out[ts(it, 128), :], x_g[:],
                          after=[t_b3])
            t_s = p.op("a", "activation", qa_tm[:], y_g[:, ts(s4, 1024)],
                       AF.Square, after=[t_y2, t_b3] + d_prev)
            t_ss = p.op("v", "reduce_sum", r1s[:], qa_tm[:], AX.X,
                        after=[t_s])
            t_mn = p.op("v", "tensor_scalar_mul", r1s[:], r1s[:],
                        1.0 / 1024.0, after=[t_ss])
            t_sq2 = p.op("a", "activation", r2s[:], r1s[:], AF.Sqrt,
                         after=[t_mn])
            t_rq2 = p.op("v", "reciprocal", r3s[:], r2s[:], after=[t_sq2])
            t_l2a = p.op("v", "tensor_scalar_mul", qa_tm[:],
                         y_g[:, ts(s4, 1024)], r3s[:], after=[t_rq2])
            t_l2f = p.op("v", "tensor_tensor", scf[:], qa_tm[:], s2_b[:],
                         ALU.mult, after=[t_l2a])
            t_l2 = p.op("v", "tensor_copy", ln2_tm[:], scf[:],
                        after=[t_l2f])
            p.dma("qp", ln2_out[ts(it, 128), :], ln2_tm[:], after=[t_l2])
            trt = None
            for c in range(8):
                trt = p.op("t", "transpose", ps_pj[:, ts(c, 128)],
                           scf[:, ts(c, 128)], identf[:],
                           after=[t_l2f, t_l2] + d_prev)
            t_lf = p.op("v", "tensor_copy", qb_tm[:], ps_pj[:, 0:1024],
                        after=[trt])
            mmr = None
            for c in range(8):
                fn = p.op if c == 7 else p.raw
                mmr = fn("t", "matmul", ps_num[:, 0:8],
                         qb_tm[:, ts(c, 128)], rw_sb[:, c * 8:c * 8 + 8],
                         start=(c == 0), stop=(c == 7), after=[t_lf])
            t_lg = p.op("v", "tensor_tensor", pr[:], ps_num[:, 0:8],
                        rb_b[:, 0:8], ALU.add, after=[mmr])
            t_m = p.op("v", "reduce_max", r1s[:], pr[:], AX.X, after=[t_lg])
            t_nm = p.op("v", "tensor_scalar_mul", r1s[:], r1s[:], -1.0,
                        after=[t_m])
            t_e = p.op("a", "activation", pr[:], pr[:], AF.Exp,
                       bias=r1s[:], after=[t_nm])
            t_sm = p.op("v", "reduce_sum", r2s[:], pr[:], AX.X, after=[t_e])
            t_ri = p.op("v", "reciprocal", r3s[:], r2s[:], after=[t_sm])
            t_p = p.op("v", "tensor_scalar_mul", pr[:], pr[:], r3s[:],
                       after=[t_ri, t_e])
            t_m1 = p.op("v", "reduce_max", r1s[:], pr[:], AX.X, after=[t_p])
            t_q1 = p.op("v", "tensor_scalar", oh1[:], pr[:], r1s[:], None,
                        ALU.is_ge, after=[t_m1, t_p])
            t_mk = p.op("v", "scalar_tensor_tensor", qb_tm[:, 0:8],
                        oh1[:], -2.0, pr[:], ALU.mult, ALU.add,
                        after=[t_q1, t_p])
            t_m2 = p.op("v", "reduce_max", r2s[:], qb_tm[:, 0:8], AX.X,
                        after=[t_mk])
            t_q2 = p.op("v", "tensor_scalar", oh2[:], pr[:], r2s[:], None,
                        ALU.is_ge, after=[t_m2, t_p])
            t_q2b = p.op("v", "tensor_tensor", oh2[:], oh2[:], oh1[:],
                         ALU.subtract, after=[t_q2, t_q1])
            t_i1 = p.op("v", "tensor_tensor", qb_tm[:, 0:8], oh1[:],
                        iota8[:], ALU.mult, after=[t_q1, t_m2] + tw)
            t_i1r = p.op("v", "reduce_sum", qb_tm[:, 8:9], qb_tm[:, 0:8],
                         AX.X, after=[t_i1])
            p.dma("qs", idx1_o[ts(it, 128), :], qb_tm[:, 8:9], after=[t_i1r])
            t_i2 = p.op("v", "tensor_tensor", qb_tm[:, 0:8], oh2[:],
                        iota8[:], ALU.mult, after=[t_q2b] + tw)
            t_i2r = p.op("v", "reduce_sum", qb_tm[:, 9:10], qb_tm[:, 0:8],
                         AX.X, after=[t_i2])
            p.dma("qs", idx2_o[ts(it, 128), :], qb_tm[:, 9:10], after=[t_i2r])
            t_sg = p.op("v", "tensor_tensor", qb_tm[:, 10:11], r1s[:],
                        r2s[:], ALU.add, after=[t_m2, t_m1])
            t_sr = p.op("v", "reciprocal", qb_tm[:, 11:12],
                        qb_tm[:, 10:11], after=[t_sg])
            t_g1 = p.op("v", "tensor_tensor", qb_tm[:, 12:13], r1s[:],
                        qb_tm[:, 11:12], ALU.mult, after=[t_sr])
            t_g2 = p.op("v", "tensor_tensor", qb_tm[:, 13:14], r2s[:],
                        qb_tm[:, 11:12], ALU.mult, after=[t_sr])
            p.dma("qs", g1_o[ts(it, 128), :], qb_tm[:, 12:13], after=[t_g1])
            p.dma("qs", g2_o[ts(it, 128), :], qb_tm[:, 13:14], after=[t_g2])
            tr1 = p.op("t", "transpose", ps_tr[0:8, 0:128], oh1[:],
                       identb[:], after=[t_q1, t_i1, t_lf])
            tr2 = p.op("t", "transpose", ps_tr[0:8, 128:256], oh2[:],
                       identb[:], after=[t_q2b, t_i2])
            t_c1 = p.op("v", "tensor_copy", oh1f[0:8, ts(it, 128)],
                        ps_tr[0:8, 0:128], after=[tr2])
            t_c2 = p.op("v", "tensor_copy", oh2f[0:8, ts(it, 128)],
                        ps_tr[0:8, 128:256], after=[tr2])
            d_prev = [t_c1, t_c2, t_b3d, t_g1, t_g2, t_i1r, t_i2r, t_q2b,
                      t_e]
        p.start_after = p.frontier()
    HW2 = min(T_, 2048)
    t_zz = p.op("v", "memset", zer8[:], 0.0)
    z8b = bass.AP(zer8[:].tensor, zer8[:].offset,
                  [[zer8[:].ap[0][0], 8], [0, HW2]])
    t_cl = p.op("v", "memset", cl8[:], 0.0)
    mm = None
    for h2 in range(T_ // HW2):
        c0 = h2 * HW2
        t_A = p.op("v", "tensor_tensor", cumx[0:8, :],
                   oh1f[0:8, c0:c0 + HW2], oh2f[0:8, c0:c0 + HW2],
                   ALU.add, after=([mm] if mm else []) + [t_cl])
        t_cs = p.op("v", "tensor_tensor_scan", cumx[0:8, :], z8b,
                    cumx[0:8, :], cl8[0:8, :], ALU.add, ALU.add,
                    after=[t_A, t_zz])
        t_cl = p.op("v", "tensor_copy", cl8[0:8, :],
                    cumx[0:8, HW2 - 1:HW2], after=[t_cs])
        if h2 == T_ // HW2 - 1:
            t_cl = p.dma("qs", tot_o, cumx[0:8, HW2 - 1:HW2],
                         after=[t_cs, t_cl])
        t_e1 = p.op("v", "tensor_tensor", cumx[0:8, :], cumx[0:8, :],
                    oh1f[0:8, c0:c0 + HW2], ALU.subtract,
                    after=[t_cs, t_cl])
        t_ex = p.op("v", "tensor_tensor", cumx[0:8, :], cumx[0:8, :],
                    oh2f[0:8, c0:c0 + HW2], ALU.subtract, after=[t_e1])
        for (ohf, dst) in ((oh1f, rk1_o), (oh2f, rk2_o)):
            for gg in range(HW2 // 512):
                t_t = p.op("v", "tensor_tensor", den_sb[0:8, :],
                           cumx[0:8, ts(gg, 512)],
                           ohf[0:8, c0 + gg * 512:c0 + gg * 512 + 512],
                           ALU.mult, after=[t_ex] + ([mm] if mm else []))
                mm = p.op("t", "matmul", ps_num[0:1, 0:512], ones8[0:8, :],
                          den_sb[0:8, :], start=True, stop=True,
                          after=[t_t])
                mm = p.op("v", "tensor_copy", den_sc[0:1, 0:512],
                          ps_num[0:1, 0:512], after=[mm])
                p.dma("qs", dst[c0 + gg * 512:c0 + gg * 512 + 512,
                                :].rearrange("t one -> one t"),
                      den_sc[0:1, 0:512], after=[mm])
    p.start_after = p.frontier()
    return p.frontier()


def emit_deltanet(p, x_in, x_out, ww_ap, wr_ap, wb_ap, decay_ap, lnb,
                  identb, identf, tag):
    T_ = x_in.shape[0]
    C = 64
    wwT = p.sbuf(f"dn_wwT_{tag}", [128, 8 * D], BF16)
    wrT = p.sbuf(f"dn_wrT_{tag}", [128, 8 * D], BF16)
    wbT = p.sbuf(f"dn_wbT_{tag}", [128, 8 * 8], BF16)
    wld = p.sbuf(f"dn_wld_{tag}", [128, D], F32)
    wlb = p.sbuf(f"dn_wlb_{tag}", [128, D], BF16)
    x_tm = p.sbuf(f"dn_x_{tag}", [128, D], F32)
    ln_tm = p.sbuf(f"dn_ln_{tag}", [128, D], F32)
    lnb16 = p.sbuf(f"dn_lnb_{tag}", [128, D], BF16)
    lnf_sub = p.sbuf(f"dn_lnf_{tag}", [128, D], BF16)
    sq_tm = p.sbuf(f"dn_sq_{tag}", [128, D], F32)
    ss8 = p.sbuf(f"dn_ss_{tag}", [128, 8], F32)
    rk_tm = p.sbuf(f"dn_rk_{tag}", [128, D], BF16)
    beta_tm = p.sbuf(f"dn_be_{tag}", [128, 8], F32)
    bde_tm = p.sbuf(f"dn_bde_{tag}", [128, 4 * 8], F32)
    rfA_tm = p.sbuf(f"dn_rfA_{tag}", [128, 8], F32)
    rk_fm = p.sbuf(f"dn_rkf_{tag}", [128, 8 * 513], BF16)
    v_tm = p.sbuf(f"dn_vt_{tag}", [128, 4 * D], BF16)
    o_fm = p.sbuf(f"dn_of_{tag}", [128, 8 * 512], BF16)
    lg_b = p.sbuf(f"dn_lgb_{tag}", [128, 8], F32)
    gp1 = p.sbuf(f"dn_gp1_{tag}", [128, 8], F32)
    de_tm = p.sbuf(f"dn_de_{tag}", [128, 8], F32)
    dw_tm = p.sbuf(f"dn_dw_{tag}", [128, 8], F32)
    rfT_tm = p.sbuf(f"dn_rfT_{tag}", [128, 8], F32)
    ct8 = p.sbuf(f"dn_ct_{tag}", [128, 8], F32)
    cmA = p.sbuf(f"dn_cmA_{tag}", [64, 8 * 64], F32)
    cmT = p.sbuf(f"dn_cmT_{tag}", [64, 8 * 64], F32)
    eyeT = p.sbuf(f"dn_eye_{tag}", [64, 8 * 64], BF16)
    debc = p.sbuf(f"dn_debc_{tag}", [128, 8 * 64], F32)
    A_sb = p.sbuf(f"dn_A_{tag}", [64, 8 * 64], BF16)
    B_sb = p.sbuf(f"dn_B_{tag}", [64, 8 * 64], BF16)
    Bt_sb = [p.sbuf(f"dn_Bt{k}_{tag}", [64, 8 * 64], BF16) for k in range(6)]
    T_sb = p.sbuf(f"dn_T_{tag}", [64, 8 * 64], BF16)
    Tt_sb = p.sbuf(f"dn_Tt_{tag}", [64, 8 * 64], BF16)
    X_sb = p.sbuf(f"dn_X_{tag}", [64, 8 * 256], BF16)
    Xp_sb = p.sbuf(f"dn_Xp_{tag}", [64, 8 * 256], BF16)
    wkdw = p.sbuf(f"dn_wkdw_{tag}", [64, 8 * 128], F32)
    wkcf = p.sbuf(f"dn_wkcf_{tag}", [128, 8 * 64], F32)
    rkde = p.sbuf(f"dn_rkde_{tag}", [128, 8 * 64], F32)
    vnew = p.sbuf(f"dn_vnew_{tag}", [64, 8 * 128], F32)
    attnT = p.sbuf(f"dn_at_{tag}", [64, 8 * 64], F32)
    S_sb = p.sbuf(f"dn_S_{tag}", [128, 8 * 128], F32)
    row1 = p.sbuf(f"dn_row_{tag}", [1, 128], F32)
    P1 = p.psum(f"dn_p1_{tag}", [64, 1024], F32)
    P2 = p.psum(f"dn_p2_{tag}", [64, 1024], F32)
    P3 = p.psum(f"dn_p3_{tag}", [128, 1024], F32)
    ps_tr = p.psum(f"dn_tr_{tag}", [128, 1024], BF16)

    SUBS = int(_os.environ.get("S1_SUBS", "9"))
    p.start_after = p.frontier()
    tw = []
    prev = []
    for (w_ap, dstT) in (((ww_ap, wwT), (wr_ap, wrT)) if SUBS >= 1 else ()):
        for r in range(8):
            tl = p.dma("qp", lnb16[:], w_ap[ts(r, 128), :], after=prev)
            trt = None
            for c in range(8):
                trt = p.op("t", "transpose", ps_tr[:, ts(c, 128)],
                           lnb16[:, ts(c, 128)], identb[:],
                           after=[tl] + prev)
            prev = [p.op("v", "tensor_copy",
                         bass.AP(dstT[:].tensor,
                                 dstT[:].offset + r * 128,
                                 [list(dstT[:].ap[0]), [D, 8], [1, 128]]),
                         ps_tr[:, 0:1024].rearrange("p (c q) -> p c q", c=8),
                         after=[trt])]
        tw += prev
    if SUBS < 2:
        tlb = p.op("v", "memset", wbT[:], 0.01)
        tw.append(tlb)
    tlb = p.dma("qp", wlb[0:8, :], wb_ap, after=[]) if SUBS >= 2 else tlb
    prevb = []
    for c in range(8 if SUBS >= 2 else 0):
        trt = p.op("t", "transpose", ps_tr[:, ts(c, 128)][0:128, 0:8],
                   wlb[0:8, ts(c, 128)], identb[0:8, 0:8],
                   after=[tlb] + prevb)
        prevb = [p.op("v", "tensor_copy", wbT[:, c * 8:c * 8 + 8],
                      ps_tr[:, ts(c, 128)][:, 0:8], after=[trt])]
    tw += prevb
    if SUBS < 1:
        tw.append(p.op("v", "memset", wwT[:], 0.01))
        tw.append(p.op("v", "memset", wrT[:], 0.01))
    if SUBS < 3:
        for bufz in (lg_b, gp1, dw_tm, rfT_tm, ct8):
            p.op("v", "memset", bufz[:], 0.01)
        for bufz in (cmA, cmT):
            p.op("v", "memset", bufz[:], 0.01)
        p.op("v", "memset", debc[:], 0.01)
    tdec = p.dma("qs", wld[0:1, 0:8], decay_ap[None, :], after=[])
    if SUBS < 3:
        t_sg = None
    t_sg = p.op("a", "activation", wld[0:1, 0:8], wld[0:1, 0:8], AF.Sigmoid,
                after=[tdec]) if SUBS >= 3 else None
    t_lg = p.op("a", "activation", wld[0:1, 8:16], wld[0:1, 0:8], AF.Ln,
                after=[t_sg])
    t_ob = p.op("v", "memset", row1[:], 1.0)
    mm = p.op("t", "matmul", P3[:, 0:8], row1[0:1, :], wld[0:1, 8:16],
              start=True, stop=True, after=[t_lg, t_ob])
    t_lgb = p.op("v", "tensor_copy", lg_b[:], P3[:, 0:8], after=[mm])
    pos64 = p.const((np.arange(128, dtype=np.float32) % 64 + 1
                     ).reshape(128, 1), f"pos64_{tag}")
    posm = p.sbuf(f"dn_pos_{tag}", [128, 1], F32)
    tpos = p.dma("qs", posm[:], pos64)
    t_g1 = p.op("v", "tensor_scalar_mul", gp1[:], lg_b[:], posm[:],
                after=[t_lgb, tpos])
    t_g1e = p.op("a", "activation", gp1[:], gp1[:], AF.Exp, after=[t_g1])
    t_rf = p.op("v", "tensor_scalar_mul", rfT_tm[:], gp1[:], -1.0,
                after=[t_g1])
    t_rfe = p.op("a", "activation", rfT_tm[:], rfT_tm[:], AF.Exp,
                 after=[t_rf])
    posw = p.const((63.0 - (np.arange(128, dtype=np.float32) % 64)
                    ).reshape(128, 1), f"posw_{tag}")
    posm2 = p.sbuf(f"dn_pos2_{tag}", [128, 1], F32)
    tpos2 = p.dma("qs", posm2[:], posw)
    t_dw = p.op("v", "tensor_scalar_mul", dw_tm[:], lg_b[:], posm2[:],
                after=[t_lgb, tpos2])
    t_dwe = p.op("a", "activation", dw_tm[:], dw_tm[:], AF.Exp, after=[t_dw])
    t_ct = p.op("v", "tensor_scalar_mul", ct8[:], lg_b[:], 64.0,
                after=[t_lgb])
    t_cte = p.op("a", "activation", ct8[:], ct8[:], AF.Exp, after=[t_ct])
    colj = p.const(np.tile(np.arange(1, 65, dtype=np.float32), (64, 1)),
                   f"colj_{tag}")
    stril = p.const(np.tril(np.ones((64, 64), np.float32), -1),
                    f"stril_{tag}")
    caus = p.const(np.tril(np.ones((64, 64), np.float32)),
                   f"caus_{tag}")
    eye_np = np.zeros((64, 8 * 64), np.float32)
    for h in range(8):
        eye_np[:, h * 64:(h + 1) * 64] = np.eye(64)
    teye = p.dma("qp", eyeT[:], p.const(eye_np, f"eye_{tag}"))
    colj_sb = p.sbuf(f"dn_colj_{tag}", [64, 64], F32)
    msk_sb = p.sbuf(f"dn_msk_{tag}", [64, 2 * 64], F32)
    tcj = p.dma("qs", colj_sb[:], colj)
    tms = [p.dma("qs", msk_sb[:, 0:64], stril),
           p.dma("qs", msk_sb[:, 64:128], caus)]
    cm_toks = []
    for h in range(8):
        lgh = lg_b[0:64, h:h + 1]
        t1 = p.op("v", "tensor_scalar_mul", cmA[:, ts(h, 64)], colj_sb[:],
                  lgh, after=[t_lgb, tcj] + cm_toks)
        t1b = p.op("v", "tensor_scalar_mul", cmA[:, ts(h, 64)],
                   cmA[:, ts(h, 64)], -1.0, after=[t1])
        t2 = p.op("a", "activation", cmA[:, ts(h, 64)], cmA[:, ts(h, 64)],
                  AF.Exp, after=[t1b])
        t3 = p.op("v", "tensor_tensor", cmA[:, ts(h, 64)], cmA[:, ts(h, 64)],
                  msk_sb[:, 0:64], ALU.mult, after=[t2] + tms)
        t3b = p.op("v", "tensor_scalar_mul", cmA[:, ts(h, 64)],
                   cmA[:, ts(h, 64)], -1.0, after=[t3])
        cm_toks = [t3b]
    t_m2a = p.op("v", "tensor_scalar", msk_sb[:, 64:128], msk_sb[:, 0:64],
                 -1.0, 1.0, ALU.mult, ALU.add, after=tms + cm_toks)
    cm2_toks = []
    for h in range(8):
        lgh = lg_b[0:64, h:h + 1]
        t4 = p.op("v", "tensor_scalar_mul", cmT[:, ts(h, 64)], colj_sb[:],
                  lgh, after=[t_lgb, tcj, t_m2a])
        t5 = p.op("a", "activation", cmT[:, ts(h, 64)], cmT[:, ts(h, 64)],
                  AF.Exp, after=[t4])
        t7 = p.op("v", "tensor_tensor", cmT[:, ts(h, 64)], cmT[:, ts(h, 64)],
                  msk_sb[:, 64:128], ALU.mult, after=[t5])
        cm2_toks.append(t7)
    colj128 = p.const(np.tile(np.arange(1, 65, dtype=np.float32),
                               (128, 1)), f"colj128_{tag}")
    colj128_sb = p.sbuf(f"dn_cj128_{tag}", [128, 64], F32)
    tcj2 = p.dma("qs", colj128_sb[:], colj128)
    db_toks = []
    for h in range(8):
        t1 = p.op("v", "tensor_scalar_mul", debc[:, ts(h, 64)],
                  colj128_sb[:], lg_b[:, h:h + 1],
                  after=[t_lgb, tcj2] + db_toks)
        db_toks = [p.op("a", "activation", debc[:, ts(h, 64)],
                        debc[:, ts(h, 64)], AF.Exp, after=[t1])]
    t_s0 = p.op("v", "memset", S_sb[:], 0.0)
    perm_np = np.zeros((2, 128, 64), np.float32)
    perm_np[0, 0:64] = np.eye(64)
    perm_np[1, 64:128] = np.eye(64)
    perm_sb = p.sbuf(f"dn_perm_{tag}", [128, 128], F32)
    bde64 = p.sbuf(f"dn_b64_{tag}", [64, 8], F32)
    tpm = [p.dma("qs", perm_sb[:, 0:64], p.const(perm_np[0], f"pm0_{tag}")),
           p.dma("qs", perm_sb[:, 64:128], p.const(perm_np[1], f"pm1_{tag}"))]
    p.start_after = p.frontier()
    sS_prev = []
    ut_prev = []
    for g in range(T_ // 512):
        rk_toks = {}
        v_toks = {}
        be_toks = {}
        ln_toks = {}
        ap_prev = []
        for s4 in range(4):
            it = g * 4 + s4
            tl = p.dma("qs", x_tm[:], x_in[ts(it, 128), :], after=ap_prev)
            t_ln = emit_layernorm_tile(p, x_tm[:], ln_tm[:], D, after=[tl],
                                       stats=lnb)
            t_lb = p.op("v", "tensor_copy", lnb16[:], ln_tm[:],
                        after=[t_ln])
            ln_toks[s4] = t_lb
            SUBA = int(_os.environ.get("S1_SUBA", "9"))
            if SUBA <= 1:
                ap_prev = [p.op("v", "tensor_copy", v_tm[:, ts(s4, 1024)],
                                ln_tm[:], after=[t_ln] + ap_prev)]
                v_toks[s4] = rk_toks[s4] = be_toks[s4] = ap_prev[0]
                continue
            t_sq = p.op("a", "activation", sq_tm[:], ln_tm[:], AF.Square,
                        after=[t_ln] + ap_prev)
            t_ss = p.op("v", "tensor_reduce", ss8[:],
                        sq_tm[:].rearrange("p (h d) -> p h d", h=8), AX.X,
                        ALU.add, after=[t_sq])
            t_nr = p.op("a", "activation", ss8[:], ss8[:], AF.Sqrt,
                        after=[t_ss])
            t_mx = p.op("v", "tensor_scalar_max", ss8[:], ss8[:], 1e-12,
                        after=[t_nr])
            t_rc = p.op("v", "reciprocal", ss8[:], ss8[:], after=[t_mx])
            t_rk = p.op("v", "tensor_tensor",
                        rk_tm[:].rearrange("p (h d) -> p h d", h=8),
                        ln_tm[:].rearrange("p (h d) -> p h d", h=8),
                        bcast16(ss8[:], reps=128), ALU.mult,
                        after=[t_rc] + ap_prev)
            trt = None
            for c in range(8):
                trt = p.op("t", "transpose", ps_tr[:, ts(c, 128)],
                           rk_tm[:, ts(c, 128)], identb[:],
                           after=[t_rk] + ap_prev)
            rkdst = bass.AP(rk_fm[:].tensor,
                            rk_fm[:].offset + 1 + s4 * 128,
                            [list(rk_fm[:].ap[0]), [513, 8], [1, 128]])
            t_rf = p.op("v", "tensor_copy", rkdst,
                        ps_tr[:, 0:1024].rearrange("p (c q) -> p c q", c=8),
                        after=[trt])
            rk_toks[s4] = t_rf
            trt2 = None
            for c in range(8):
                trt2 = p.op("t", "transpose", ps_tr[:, ts(c, 128)],
                            lnb16[:, ts(c, 128)], identb[:],
                            after=[t_lb, t_rf])
            t_lnf = p.op("v", "tensor_copy", lnf_sub[:], ps_tr[:, 0:1024],
                         after=[trt2])
            if SUBA <= 2:
                ap_prev = [p.op("v", "tensor_copy", v_tm[:, ts(s4, 1024)],
                                ln_tm[:], after=[t_rf] + ap_prev)]
                v_toks[s4] = be_toks[s4] = ap_prev[0]
                continue
            mm = None
            for c in range(8):
                fn = p.op if c == 7 else p.raw
                mm = fn("t", "matmul", P3[:, 0:8], lnf_sub[:, ts(c, 128)],
                        wbT[:, c * 8:c * 8 + 8], start=(c == 0),
                        stop=(c == 7), after=[t_lnf] + tw + ap_prev)
            t_be = p.op("a", "activation", beta_tm[:], P3[:, 0:8],
                        AF.Sigmoid, after=[mm])
            be_toks[s4] = t_be
            t_bd = p.op("v", "tensor_tensor", bde_tm[:, ts(s4, 8)],
                        beta_tm[:], gp1[:], ALU.mult,
                        after=[t_be, t_g1e])
            if SUBA <= 3:
                ap_prev = [p.op("v", "tensor_copy", v_tm[:, ts(s4, 1024)],
                                ln_tm[:], after=[t_bd] + ap_prev)]
                v_toks[s4] = ap_prev[0]
                continue
            mm = None
            for n2 in range(2):
                for c in range(8):
                    fn = p.op if (c == 7 and n2 == 1) else p.raw
                    mm = fn("t", "matmul", P3[:, ts(n2, 512)],
                            lnf_sub[:, ts(c, 128)],
                            wwT[:, c * D + n2 * 512:c * D + n2 * 512 + 512],
                            start=(c == 0), stop=(c == 7),
                            after=[t_lnf, t_be] + tw)
            v_toks[s4] = p.op("v", "tensor_tensor",
                              v_tm[:, ts(s4, 1024)].rearrange(
                                  "p (h d) -> p h d", h=8),
                              P3[:, 0:1024].rearrange(
                                  "p (h d) -> p h d", h=8),
                              bcast16(beta_tm[:], reps=128), ALU.mult,
                              after=[mm, t_bd])
            ap_prev = [t_rf, v_toks[s4], t_bd]
            p.start_after = p.frontier()
        if int(_os.environ.get("S1_SUBA", "9")) <= 1:
            sv_last = []
            p.start_after = p.frontier()
        if g == 0 and int(_os.environ.get("S1_SUBA", "9")) > 1:
            z_tok = p.op("v", "memset",
                         bass.AP(rk_fm[:].tensor, rk_fm[:].offset,
                                 [list(rk_fm[:].ap[0]), [513, 8], [1, 1]]),
                         0.0, after=ap_prev)
        elif int(_os.environ.get("S1_SUBA", "9")) > 1:
            z_tok = p.op("v", "tensor_copy",
                         bass.AP(rk_fm[:].tensor, rk_fm[:].offset,
                                 [list(rk_fm[:].ap[0]), [513, 8], [1, 1]]),
                         bass.AP(rk_fm[:].tensor,
                                 rk_fm[:].offset + 512,
                                 [list(rk_fm[:].ap[0]), [513, 8], [1, 1]]),
                         after=ap_prev + sv_last)
        if int(_os.environ.get("S1_SUBA", "9")) > 1:
            sv_last = [z_tok]
        S1S = int(_os.environ.get("S1_STAGE", "9"))
        for cc in range(8 if S1S >= 2 else 0):
            s4, hh = cc // 2, cc % 2
            b0 = 64 * hh
            def rkv(h, shift):
                a = rk_fm[:]
                return bass.AP(a.tensor, a.offset + h * 513 + 1 + cc * 64
                               - shift, [list(a.ap[0]), [1, 64]])
            mmP = p.op("t", "matmul", P1[0:64, 512:520],
                       perm_sb[:, 64 * hh:64 * hh + 64],
                       bde_tm[:, ts(s4, 8)],
                       start=True, stop=True,
                       after=[t_bd] + tpm + ut_prev + sS_prev)
            t_b64 = p.op("v", "tensor_copy", bde64[:], P1[0:64, 512:520],
                         after=[mmP])
            p.start_after = p.frontier()
            tr_toks = []
            for h in range(8):
                trt = p.op("t", "transpose", ps_tr[0:64, ts(h, 128)],
                           rkv(h, 1), identb[:],
                           after=[rk_toks[s4], z_tok] + sS_prev)
                tr_toks.append(trt)
            ev1 = []
            for h in range(8):
                ev1.append(p.op("v", "tensor_scalar", wkdw[:, ts(h, 128)],
                                ps_tr[0:64, h * 128:h * 128 + 128],
                                dw_tm[0:64, h:h + 1], None, ALU.mult,
                                after=tr_toks + ut_prev + [t_dwe]))
                ev1.append(p.op("v", "tensor_scalar",
                                X_sb[:, h * 256 + 128:h * 256 + 256],
                                ps_tr[0:64, h * 128:h * 128 + 128],
                                bde64[0:64, h:h + 1], None, ALU.mult,
                                after=tr_toks + [t_b64] + ut_prev))
            xv = X_sb[:]
            xdst = bass.AP(xv.tensor, xv.offset,
                           [list(xv.ap[0]), [256, 8], [1, 128]])
            vsrc0 = v_tm[b0:b0 + 64, s4 * 1024:s4 * 1024 + 1024]
            vsrc = bass.AP(vsrc0.tensor, vsrc0.offset,
                           [list(vsrc0.ap[0]), [128, 8], [1, 128]])
            ev1.append(p.op("v", "tensor_copy", xdst, vsrc,
                            after=[v_toks[s4]] + ut_prev))
            mmA = None
            for h in range(8):
                fn = p.op if h == 7 else p.raw
                mmA = fn("t", "matmul", P1[0:64, ts(h, 64)],
                         rkv(h, 1), rkv(h, 1), start=True, stop=True,
                         after=[rk_toks[s4], z_tok] + ut_prev + sS_prev)
            a_toks = []
            for h in range(8):
                a_toks.append(p.op("v", "scalar_tensor_tensor",
                                   A_sb[:, ts(h, 64)], P1[0:64, ts(h, 64)],
                                   bde64[0:64, h:h + 1],
                                   cmA[:, ts(h, 64)], ALU.mult, ALU.mult,
                                   after=[mmA, t_b64] + cm_toks))
            trA = None
            for h in range(8):
                trA = p.op("t", "transpose", ps_tr[0:64, ts(h, 64)],
                           A_sb[:, ts(h, 64)], identb[0:64, 0:64],
                           after=a_toks + ev1)
            t_At = p.op("v", "tensor_copy", Bt_sb[0][:],
                        ps_tr[0:64, 0:512], after=[trA])
            t_B = p.op("v", "tensor_copy", B_sb[:], A_sb[:], after=a_toks)
            t_T = p.op("v", "tensor_tensor", T_sb[:], A_sb[:], eyeT[:],
                       ALU.add, after=a_toks + [teye])
            for k in range(1, 6):
                mmB = None
                for h in range(8):
                    p.raw("t", "matmul", P1[0:64, ts(h, 64)],
                          Bt_sb[k - 1][:, ts(h, 64)], B_sb[:, ts(h, 64)],
                          start=True, stop=True,
                          after=[t_At, t_B, t_T])
                    fn = p.op if h == 7 else p.raw
                    mmB = fn("t", "matmul", P2[0:64, ts(h, 64)],
                             B_sb[:, ts(h, 64)], Bt_sb[k - 1][:, ts(h, 64)],
                             start=True, stop=True)
                t_B = p.op("v", "tensor_copy", B_sb[:], P1[0:64, 0:512],
                           after=[mmB])
                t_At = p.op("v", "tensor_copy", Bt_sb[k][:],
                            P2[0:64, 0:512], after=[mmB])
                mmT = None
                for h in range(8):
                    p.raw("t", "matmul", P1[0:64, ts(h, 64)], eyeT[:, ts(h, 64)],
                          T_sb[:, ts(h, 64)], start=True, stop=False,
                          after=[t_B, t_At])
                    fn = p.op if h == 7 else p.raw
                    mmT = fn("t", "matmul", P1[0:64, ts(h, 64)],
                             Bt_sb[k][:, ts(h, 64)], T_sb[:, ts(h, 64)],
                             start=False, stop=True)
                t_T = p.op("v", "tensor_copy", T_sb[:], P1[0:64, 0:512],
                           after=[mmT])
            trT = None
            for h in range(8):
                trT = p.op("t", "transpose", ps_tr[0:64, ts(h, 64)],
                           T_sb[:, ts(h, 64)], identb[0:64, 0:64],
                           after=[t_T])
            t_Tt = p.op("v", "tensor_copy", Tt_sb[:], ps_tr[0:64, 0:512],
                        after=[trT])
            xp_toks = []
            for hf in range(2):
                mmX = None
                for h4 in range(4):
                    h = hf * 4 + h4
                    fn = p.op if h4 == 3 else p.raw
                    mmX = fn("t", "matmul", P2[0:64, ts(h4, 256)],
                             Tt_sb[:, ts(h, 64)],
                             X_sb[:, h * 256:h * 256 + 256],
                             start=True, stop=True,
                             after=[t_Tt] + ev1 + xp_toks)
                xp_toks.append(p.op("v", "tensor_copy",
                                    Xp_sb[:, hf * 1024:hf * 1024 + 1024],
                                    P2[0:64, 0:1024], after=[mmX]))
            ut_prev = xp_toks
            if S1S == 2:
                ut_prev = xp_toks
                sS_prev = xp_toks
                continue
            p.start_after = p.frontier()
            trW = None
            for h in range(8):
                trW = p.op("t", "transpose", ps_tr[:, ts(h, 128)][:, 0:64],
                           Xp_sb[0:64, h * 256 + 128:h * 256 + 256],
                           identb[0:64, 0:64], after=xp_toks + sS_prev)
            t_wf = p.op("v", "tensor_copy", wkcf[:],
                        ps_tr[:, 0:1024].rearrange(
                            "p (c q) -> p c q", c=8)[:, :, 0:64],
                        after=[trW])
            rk8 = bass.AP(rk_fm[:].tensor, rk_fm[:].offset + 1 + cc * 64,
                          [list(rk_fm[:].ap[0]), [513, 8], [1, 64]])
            t_rd = p.op("v", "tensor_tensor",
                        rkde[:].rearrange("p (h d) -> p h d", h=8), rk8,
                        debc[:].rearrange("p (h d) -> p h d", h=8),
                        ALU.mult, after=[rk_toks[s4]] + db_toks + sS_prev)
            mmV = None
            for h in range(8):
                fn = p.op if h == 7 else p.raw
                mmV = fn("t", "matmul", P3[0:64, ts(h, 128)],
                         wkcf[:, ts(h, 64)], S_sb[:, ts(h, 128)],
                         start=True, stop=True,
                         after=[t_wf, t_s0] + sS_prev)
            t_vn = p.op("v", "tensor_tensor", vnew[:],
                        bass.AP(Xp_sb[:].tensor, Xp_sb[:].offset,
                                [list(Xp_sb[:].ap[0]), [256, 8], [1, 128]]),
                        P3[0:64, 0:1024], ALU.subtract,
                        after=[mmV] + xp_toks)
            p.start_after = p.frontier()
            mmAt = None
            for h in range(8):
                fn = p.op if h == 7 else p.raw
                mmAt = fn("t", "matmul", P1[0:64, ts(h, 64)],
                          rkv(h, 1), rkv(h, 0), start=True, stop=True,
                          after=[t_vn] + cm2_toks)
            at_toks = []
            for h in range(8):
                at_toks.append(p.op("v", "scalar_tensor_tensor",
                                    attnT[:, ts(h, 64)], P1[0:64, ts(h, 64)],
                                    rfT_tm[0:64, h:h + 1],
                                    cmT[:, ts(h, 64)], ALU.mult, ALU.mult,
                                    after=[mmAt, t_rfe]))
            p.start_after = p.frontier()
            if _os.environ.get("S1_DBG") and g == 0 and cc == 0:
                dbgs = {}
                for nm, src, tok in (
                        ("rk", rk_tm[:], rk_toks[s4]),
                        ("vtm", v_tm[:, 0:1024], v_toks[0]),
                        ("A", A_sb[:], a_toks[-1]),
                        ("Tm", T_sb[:], t_T),
                        ("Xp", Xp_sb[:], ut_prev[-1]),
                        ("vnew", vnew[:], t_vn),
                        ("at", attnT[:], at_toks[-1]),
                        ("beta", beta_tm[:], be_toks[0]),
                        ("wkdw", wkdw[:], ev1[0]),
                        ("Xin", X_sb[:], ev1[-1])):
                    dd = p.dram(f"dbg_{nm}", [src.shape[0],
                                              src.shape[1]], F32,
                                kind="ExternalOutput")
                    p.dma("qp", dd, src, after=[tok] + at_toks)
            mmO = None
            for h in range(8):
                p.raw("t", "matmul", P3[:, ts(h, 64)][0:128, 0:64],
                      S_sb[:, ts(h, 128)], rkde[:, ts(h, 64)],
                      start=True, stop=False,
                      after=[t_rd, t_vn] + at_toks)
                fn = p.op if h == 7 else p.raw
                mmO = fn("t", "matmul", P3[:, ts(h, 64)][0:128, 0:64],
                         vnew[:, ts(h, 128)], attnT[:, ts(h, 64)],
                         start=False, stop=True)
            t_o = p.op("v", "tensor_copy",
                       bass.AP(o_fm[:].tensor, o_fm[:].offset + cc * 64,
                               [list(o_fm[:].ap[0]), [512, 8], [1, 64]]),
                       P3[:, 0:512], after=[mmO])
            p.start_after = p.frontier()
            mmS = None
            for h in range(8):
                fn = p.op if h == 7 else p.raw
                mmS = fn("t", "matmul", P3[:, ts(h, 128)],
                         wkdw[:, ts(h, 128)], vnew[:, ts(h, 128)],
                         start=True, stop=True,
                         after=[t_vn, t_o] + ev1)
            sS_prev = []
            for h in range(8):
                sS_prev.append(p.op("v", "scalar_tensor_tensor",
                                    S_sb[:, ts(h, 128)], S_sb[:, ts(h, 128)],
                                    ct8[:, h:h + 1], P3[:, ts(h, 128)],
                                    ALU.mult, ALU.add,
                                    after=[mmS, t_cte]))
            p.start_after = p.frontier()
        if S1S <= 2:
            d2_prev = []
            for s4 in range(4):
                it = g * 4 + s4
                st = p.dma("qp", x_out[ts(it, 128), :],
                           v_tm[:, ts(s4, 1024)], after=p.frontier())
            p.start_after = p.frontier()
            continue
        d2_prev = list(sS_prev)
        for s4 in range(4):
            it = g * 4 + s4
            mm = None
            for n2 in range(2):
                for c in range(8):
                    fn = p.op if (c == 7 and n2 == 1) else p.raw
                    mm = fn("t", "matmul", P3[:, ts(n2, 512)],
                            o_fm[:, c * 512 + s4 * 128:
                                 c * 512 + s4 * 128 + 128],
                            wrT[:, c * D + n2 * 512:c * D + n2 * 512 + 512],
                            start=(c == 0), stop=(c == 7),
                            after=sS_prev + [t_o] + d2_prev + tw)
            tlx = p.dma("qs", x_tm[:], x_in[ts(it, 128), :],
                        after=d2_prev)
            t_ln2 = emit_layernorm_tile(p, x_tm[:], ln_tm[:], D,
                                        after=[tlx], stats=lnb)
            t_z1 = p.op("v", "tensor_tensor", ln_tm[:], ln_tm[:],
                        P3[:, 0:1024], ALU.add, after=[mm, t_ln2])
            t_z2 = p.op("v", "tensor_tensor", x_tm[:], x_tm[:], ln_tm[:],
                        ALU.add, after=[t_z1])
            d2_prev = [p.dma("qs", x_out[ts(it, 128), :], x_tm[:],
                             after=[t_z2]), t_z2]
            p.start_after = p.frontier()
        p.start_after = p.frontier()
    return p.frontier()


CAP = 5120


def emit_moe_expert(p, ln2_ap, idx1_ap, idx2_ap, rk1_ap, rk2_ap, tote_ap,
                    eid_ap, w1_ap, b1_ap, w2_ap, b2_ap, h_out, identb,
                    identf, tag):
    NTOK = ln2_ap.shape[0]
    w1sb = p.sbuf(f"me_w1_{tag}", [128, 8 * 4096], BF16)
    w2sb = p.sbuf(f"me_w2_{tag}", [128, 32 * 1024], BF16)
    b1f = p.sbuf(f"me_b1_{tag}", [128, 32], F32)
    b2b = p.sbuf(f"me_b2_{tag}", [128, D], F32)
    row32 = p.sbuf(f"me_r32_{tag}", [32, 128], F32)
    vecs = p.sbuf(f"me_vec_{tag}", [128, 4 * 32], F32)
    tote = p.sbuf(f"me_te_{tag}", [1, 8], F32)
    base8 = p.sbuf(f"me_b8_{tag}", [1, 8], F32)
    eid_b = p.sbuf(f"me_eid_{tag}", [128, 1], F32)
    base_b = p.sbuf(f"me_bb_{tag}", [128, 1], F32)
    sl = p.sbuf(f"me_sl_{tag}", [128, 8], F32)
    slu = p.sbuf(f"me_slu_{tag}", [128, 1], U32)
    xt = p.sbuf(f"me_xt_{tag}", [128, D], BF16)
    xf = p.sbuf(f"me_xf_{tag}", [128, 8 * 512], BF16)
    hf = p.sbuf(f"me_hf_{tag}", [128, 32 * 512], BF16)
    ot = p.sbuf(f"me_ot_{tag}", [128, D], BF16)
    ones1 = p.sbuf(f"me_on_{tag}", [1, 128], F32)
    ps_a = p.psum(f"me_pa_{tag}", [128, 2048], F32)
    ps_b = p.psum(f"me_pb_{tag}", [128, 1024], F32)
    ps_tr = p.psum(f"me_tr_{tag}", [128, 1024], BF16)
    ps_tf = p.psum(f"me_tf_{tag}", [128, 512], F32)
    Xg = p.dram(f"me_xg_{tag}", [CAP + 128, D], BF16)
    bigc = p.sbuf(f"me_big_{tag}", [128, 1], F32)
    gar = p.sbuf(f"me_gar_{tag}", [128, 1], F32)
    bc_reg = p.eng("p").to_reg(CAP + 127)

    p.start_after = p.frontier()
    t_big = p.op("p", "iota", bigc[:], pattern=[[0, 1]], base=100000,
                 channel_multiplier=128,
                 allow_small_or_imprecise_dtypes=True)
    t_gar = p.op("p", "iota", gar[:], pattern=[[0, 1]], base=CAP,
                 channel_multiplier=1,
                 allow_small_or_imprecise_dtypes=True)
    tw = [p.dma("qp", w1sb[:].rearrange("p (c i) -> p c i", c=8),
                w1_ap.rearrange("(c p) i -> p c i", p=128)),
          p.dma("qp", w2sb[:].rearrange("p (c i) -> p c i", c=32),
                w2_ap.rearrange("(c p) i -> p c i", p=128))]
    t_on = p.op("v", "memset", ones1[:], 1.0)
    tb1 = p.dma("qs", row32[:], b1_ap.rearrange("(c p) -> c p", c=32))
    tr1 = p.op("t", "transpose", ps_tf[:, 0:32], row32[:],
               identf[0:32, 0:32], after=[tb1])
    tb1b = p.op("v", "tensor_copy", b1f[:], ps_tf[:, 0:32], after=[tr1])
    tb2 = p.dma("qs", row32[0:1, :], b2_ap[None, 0:128])
    mmb = []
    for n0 in range(0, D, 512):
        tb2n = p.dma("qs", row32[0:1, :], b2_ap[None, n0:n0 + 128]) \
            if False else None
    b2row = p.sbuf(f"me_b2r_{tag}", [1, D], F32)
    tb2r = p.dma("qs", b2row[:], b2_ap[None, :])
    mm = None
    for n0 in range(0, D, 512):
        mm = p.op("t", "matmul", ps_b[:, n0:n0 + 512], ones1[0:1, :],
                  b2row[0:1, n0:n0 + 512], start=True, stop=True,
                  after=[tb2r, t_on])
    tb2b = p.op("v", "tensor_copy", b2b[:], ps_b[:, 0:1024], after=[mm])
    tte = p.dma("qs", tote[:], tote_ap[None, :])
    teid = p.dma("qs", row32[0:1, 0:1], eid_ap[None, :], after=[tr1])
    mm = p.op("t", "matmul", ps_tf[:, 0:1], ones1[0:1, :], row32[0:1, 0:1],
              start=True, stop=True, after=[teid, t_on, tb1b])
    t_eb = p.op("v", "tensor_copy", eid_b[:], ps_tf[:, 0:1], after=[mm])
    t_z = p.op("v", "memset", base8[:], 0.0)
    t_cs = p.op("v", "tensor_tensor_scan", base8[:], base8[:], tote[:],
                0.0, ALU.add, ALU.add, after=[tte, t_z])
    t_ex = p.op("v", "tensor_tensor", base8[:], base8[:], tote[:],
                ALU.subtract, after=[t_cs])
    t_zx = p.op("v", "memset", xt[:], 0.0)
    zt = []
    for i in range((CAP + 128) // 128):
        zt = [p.dma("qs", Xg[ts(i, 128), :], xt[:], after=[t_zx])]
    p.start_after = p.frontier()
    MES = int(_os.environ.get("ME_STAGE", "9"))
    sc_prev = []
    for r in range(NTOK // 4096 if MES != 1 else 0):
        mm = p.op("t", "matmul", ps_tf[:, 0:1], ones1[0:1, :],
                  base8[0:1, r:r + 1], start=True, stop=True,
                  after=[t_ex, t_eb] + sc_prev)
        t_bb = p.op("v", "tensor_copy", base_b[:], ps_tf[:, 0:1],
                    after=[mm])
        v_toks = [t_bb]
        prev_tr = None
        for (vi, vap) in enumerate((idx1_ap, idx2_ap, rk1_ap, rk2_ap)):
            tv = p.dma("qs", row32[:],
                       vap[r * 4096:(r + 1) * 4096, :].rearrange(
                           "(c p) one -> c (p one)", c=32),
                       after=sc_prev + ([prev_tr] if prev_tr else []))
            trv = p.op("t", "transpose", ps_tf[:, 0:32], row32[:],
                       identf[0:32, 0:32], after=[tv] + v_toks)
            prev_tr = trv
            v_toks.append(p.op("v", "tensor_copy", vecs[:, ts(vi, 32)],
                               ps_tf[:, 0:32], after=[trv]))
        sc_prev = []
        for j in range(32):
            it = r * 32 + j
            t1 = p.op("v", "tensor_scalar", sl[:, 0:1],
                      vecs[:, 0 * 32 + j:0 * 32 + j + 1], eid_b[:], None,
                      ALU.is_equal, after=v_toks + [t_eb, t_bb] + sc_prev)
            t2 = p.op("v", "tensor_tensor", sl[:, 1:2],
                      vecs[:, 2 * 32 + j:2 * 32 + j + 1], base_b[:],
                      ALU.add, after=v_toks + [t_bb] + sc_prev)
            t3 = p.op("v", "tensor_tensor", sl[:, 1:2], sl[:, 1:2],
                      bigc[:], ALU.subtract, after=[t2, t_big])
            t4 = p.op("v", "tensor_scalar", sl[:, 1:2], sl[:, 1:2],
                      sl[:, 0:1], None, ALU.mult, after=[t3, t1])
            t4 = p.op("v", "tensor_tensor", sl[:, 1:2], sl[:, 1:2],
                      bigc[:], ALU.add, after=[t4])
            t5 = p.op("v", "tensor_scalar", sl[:, 2:3],
                      vecs[:, 1 * 32 + j:1 * 32 + j + 1], eid_b[:], None,
                      ALU.is_equal, after=v_toks + [t_eb] + sc_prev)
            t6 = p.op("v", "tensor_tensor", sl[:, 3:4],
                      vecs[:, 3 * 32 + j:3 * 32 + j + 1], base_b[:],
                      ALU.add, after=v_toks + [t_bb] + sc_prev)
            t7 = p.op("v", "tensor_tensor", sl[:, 3:4], sl[:, 3:4],
                      bigc[:], ALU.subtract, after=[t6, t_big])
            t8 = p.op("v", "tensor_scalar", sl[:, 3:4], sl[:, 3:4],
                      sl[:, 2:3], None, ALU.mult, after=[t7, t5])
            t8 = p.op("v", "tensor_tensor", sl[:, 3:4], sl[:, 3:4],
                      bigc[:], ALU.add, after=[t8])
            t9 = p.op("v", "tensor_tensor", sl[:, 4:5], sl[:, 1:2],
                      sl[:, 3:4], ALU.min, after=[t4, t8])
            tc1 = p.op("v", "tensor_scalar", sl[:, 5:6], sl[:, 4:5],
                       float(CAP) - 0.5, None, ALU.is_lt, after=[t9])
            tc2 = p.op("v", "tensor_tensor", sl[:, 6:7], sl[:, 4:5],
                       gar[:], ALU.subtract, after=[t9, t_gar])
            tc3 = p.op("v", "tensor_tensor", sl[:, 6:7], sl[:, 6:7],
                       sl[:, 5:6], ALU.mult, after=[tc1, tc2])
            tc4 = p.op("v", "tensor_tensor", sl[:, 4:5], sl[:, 6:7],
                       gar[:], ALU.add, after=[tc3])
            tu = p.op("v", "tensor_copy", slu[:], sl[:, 4:5], after=[tc4])
            tx = p.dma("qs", xt[:], ln2_ap[ts(it, 128), :], after=sc_prev)
            tsc = p.op("qp", "indirect_dma_start", Xg[0:128, :],
                       bass.IndirectOffsetOnAxis(slu[:], 0), xt[:], None,
                       bounds_check=bc_reg, after=[tu, tx])
            sc_prev = [tsc, tu]
    p.start_after = p.frontier()
    for gidx in range(CAP // 512 if MES != 2 else 0):
        xf_toks = []
        for s4 in range(4):
            tl = p.dma("qs", xt[:], Xg[ts(gidx * 4 + s4, 128), :],
                       after=xf_toks)
            trt = None
            for c in range(8):
                trt = p.op("t", "transpose", ps_tr[:, ts(c, 128)],
                           xt[:, ts(c, 128)], identb[:],
                           after=[tl] + xf_toks)
            xdst = bass.AP(xf[:].tensor, xf[:].offset + s4 * 128,
                           [list(xf[:].ap[0]), [512, 8], [1, 128]])
            xf_toks = [p.op("v", "tensor_copy", xdst,
                            ps_tr[:, 0:1024].rearrange(
                                "p (c q) -> p c q", c=8), after=[trt])]
        g_toks = []
        for fb in range(8):
            mm = None
            for f4 in range(4):
                f = fb * 4 + f4
                for c in range(8):
                    fn = p.op if (c == 7 and f4 == 3) else p.raw
                    mm = fn("t", "matmul", ps_a[:, ts(f4, 512)],
                            w1sb[:, c * 4096 + f * 128:
                                 c * 4096 + f * 128 + 128],
                            xf[:, ts(c, 512)], start=(c == 0),
                            stop=(c == 7),
                            after=xf_toks + tw + g_toks)
            for f4 in range(4):
                f = fb * 4 + f4
                g_toks.append(p.op("a", "activation",
                                   hf[:, f * 512:f * 512 + 512],
                                   ps_a[:, ts(f4, 512)], AF.Gelu,
                                   bias=b1f[:, f:f + 1],
                                   after=[mm, tb1b]))
        o_prev = []
        for s4 in range(4):
            mm = None
            for n2 in range(2):
                for c in range(32):
                    fn = p.op if (c == 31 and n2 == 1) else p.raw
                    mm = fn("t", "matmul", ps_b[:, ts(n2, 512)],
                            hf[:, c * 512 + s4 * 128:
                               c * 512 + s4 * 128 + 128],
                            w2sb[:, c * 1024 + n2 * 512:
                                 c * 1024 + n2 * 512 + 512],
                            start=(c == 0), stop=(c == 31),
                            after=g_toks + tw + o_prev)
            t_o = p.op("v", "tensor_tensor", ot[:], ps_b[:, 0:1024],
                       b2b[:], ALU.add, after=[mm, tb2b])
            o_prev = [p.dma("qs", h_out[ts(gidx * 4 + s4, 128), :], ot[:],
                            after=[t_o]), t_o]
        p.start_after = p.frontier()
    if MES == 2:
        xz = p.dma("qs", xt[:], Xg[0:128, :], after=p.frontier())
        p.dma("qp", h_out[0:128, :], xt[:], after=[xz])
        p.start_after = p.frontier()
    return p.frontier()



def emit_moe_combine(p, base3_ap, h_all_ap, idx1_ap, idx2_ap, rk1_ap,
                     rk2_ap, g1_ap, g2_ap, tot_all_ap, cid_ap, x3_out,
                     identf, tag, dbg_ap=None):
    T_ = base3_ap.shape[0]
    tot8 = p.sbuf(f"mc_t8_{tag}", [8, 8], F32)
    cid_b = p.sbuf(f"mc_cb_{tag}", [8, 1], F32)
    io8 = p.sbuf(f"mc_io_{tag}", [8, 1], F32)
    basev = p.sbuf(f"mc_bv_{tag}", [1, 8], F32)
    base_bc = p.sbuf(f"mc_bb_{tag}", [128, 8], F32)
    iota_b = p.sbuf(f"mc_ib_{tag}", [128, 8], F32)
    vecs = p.sbuf(f"mc_vec_{tag}", [128, 6 * 32], F32)
    row32 = p.sbuf(f"mc_r32_{tag}", [32, 128], F32)
    eq8 = p.sbuf(f"mc_eq_{tag}", [128, 8], F32)
    sl = p.sbuf(f"mc_sl_{tag}", [128, 8], F32)
    slu = [p.sbuf(f"mc_slu0_{tag}", [128, 1], U32),
           p.sbuf(f"mc_slu1_{tag}", [128, 1], U32)]
    ones1 = p.sbuf(f"mc_on_{tag}", [1, 128], F32)
    xt = p.sbuf(f"mc_xt_{tag}", [128, D], F32)
    gt = [p.sbuf(f"mc_g1_{tag}", [128, D], BF16),
          p.sbuf(f"mc_g2_{tag}", [128, D], BF16)]
    ps_tf = p.psum(f"mc_tf_{tag}", [128, 512], F32)
    bc_reg2 = p.eng("p").to_reg(8 * CAP + 127)
    p.start_after = p.frontier()
    t_on = p.op("v", "memset", ones1[:], 1.0)
    tt8 = p.dma("qs", tot8[:], tot_all_ap)
    tcid = p.dma("qs", row32[0:1, 0:1], cid_ap[None, :])
    mm = p.op("t", "matmul", ps_tf[0:8, 0:1], ones1[0:1, 0:8],
              row32[0:1, 0:1], start=True, stop=True, after=[tcid, t_on])
    t_cb = p.op("v", "tensor_copy", cid_b[:], ps_tf[0:8, 0:1], after=[mm])
    t_io = p.op("p", "iota", io8[:], pattern=[[0, 1]], base=0,
                channel_multiplier=1, allow_small_or_imprecise_dtypes=True)
    t_mk = p.op("v", "tensor_scalar", io8[:], io8[:], cid_b[:, 0:1]
                if False else cid_b[:], None, ALU.is_lt,
                after=[t_io, t_cb])
    t_tm = p.op("v", "tensor_scalar", tot8[:], tot8[:], io8[:], None,
                ALU.mult, after=[t_mk, tt8])
    mm = p.op("t", "matmul", ps_tf[0:1, 0:8], io8[:] if False else
              bass.AP(io8[:].tensor, io8[:].offset,
                      [list(io8[:].ap[0]), [0, 1]]), tot8[:],
              start=True, stop=True, after=[t_tm, t_on]) if False else None
    onc = p.sbuf(f"mc_onc_{tag}", [8, 1], F32)
    t_onc = p.op("v", "memset", onc[:], 1.0)
    mm = p.op("t", "matmul", ps_tf[0:1, 0:8], onc[:], tot8[:],
              start=True, stop=True, after=[t_tm, t_onc, t_cb])
    t_bv = p.op("v", "tensor_copy", basev[:], ps_tf[0:1, 0:8], after=[mm])
    mm = p.op("t", "matmul", ps_tf[:, 0:8], ones1[0:1, :], basev[0:1, :],
              start=True, stop=True, after=[t_bv])
    t_bb = p.op("v", "tensor_copy", base_bc[:], ps_tf[:, 0:8], after=[mm])
    t_ib = p.op("p", "iota", iota_b[:], pattern=[[1, 8]], base=0,
                channel_multiplier=0, allow_small_or_imprecise_dtypes=True)
    p.start_after = p.frontier()
    out_prev = []
    for r in range(T_ // 4096):
        v_toks = [t_bb]
        prev_tr = None
        for (vi, vap) in enumerate((idx1_ap, idx2_ap, rk1_ap, rk2_ap,
                                    g1_ap, g2_ap)):
            tv = p.dma("qs", row32[:],
                       vap[r * 4096:(r + 1) * 4096, :].rearrange(
                           "(c p) one -> c (p one)", c=32),
                       after=out_prev + ([prev_tr] if prev_tr else []))
            trv = p.op("t", "transpose", ps_tf[:, 0:32], row32[:],
                       identf[0:32, 0:32], after=[tv] + v_toks)
            prev_tr = trv
            v_toks.append(p.op("v", "tensor_copy", vecs[:, ts(vi, 32)],
                               ps_tf[:, 0:32], after=[trv]))
        for j in range(32):
            it = r * 32 + j
            dbg_toks = []
            tl = p.dma("qs", xt[:], base3_ap[ts(it, 128), :],
                       after=out_prev)
            gath = []
            for (ch, (iv, rv)) in enumerate(((0, 2), (1, 3))):
                t1 = p.op("v", "tensor_scalar", eq8[:], iota_b[:],
                          vecs[:, iv * 32 + j:iv * 32 + j + 1], None,
                          ALU.is_equal, after=v_toks + [t_ib] + out_prev)
                t2 = p.op("v", "tensor_tensor", eq8[:], eq8[:],
                          base_bc[:], ALU.mult, after=[t1, t_bb])
                t3 = p.op("v", "reduce_sum", sl[:, 0:1], eq8[:], AX.X,
                          after=[t2])
                t4 = p.op("v", "tensor_tensor", sl[:, 1:2],
                          vecs[:, rv * 32 + j:rv * 32 + j + 1], sl[:, 0:1],
                          ALU.add, after=[t3] + v_toks)
                t5 = p.op("v", "tensor_scalar", sl[:, 2:3], sl[:, 1:2],
                          float(CAP) - 0.5, None, ALU.is_lt, after=[t4])
                t6 = p.op("v", "tensor_scalar", sl[:, 3:4],
                          vecs[:, iv * 32 + j:iv * 32 + j + 1], float(CAP),
                          None, ALU.mult, after=v_toks + out_prev)
                t7 = p.op("v", "tensor_tensor", sl[:, 3:4], sl[:, 3:4],
                          sl[:, 1:2], ALU.add, after=[t6, t4])
                t8 = p.op("v", "tensor_scalar", sl[:, 3:4], sl[:, 3:4],
                          -float(8 * CAP), None, ALU.add, after=[t7])
                t9 = p.op("v", "tensor_scalar", sl[:, 3:4], sl[:, 3:4],
                          sl[:, 2:3], float(8 * CAP), ALU.mult, ALU.add,
                          after=[t8, t5])
                tu = p.op("v", "tensor_copy", slu[ch][:],
                          sl[:, 3:4], after=[t9])
                if dbg_ap is not None:
                    dbg_toks.append(p.dma(
                        "qs", dbg_ap[ts(it, 128), ch:ch + 1], sl[:, 3:4],
                        after=[t9], allow_slow_non_contiguous=True))
                tz = p.op("v", "memset", gt[ch][:], 0.0,
                          after=[tu] + out_prev)
                if int(_os.environ.get("MC_STAGE", "9")) == 1:
                    gath.append(tz)
                else:
                    gath.append(p.op("qp", "indirect_dma_start", gt[ch][:],
                                     None, h_all_ap,
                                     bass.IndirectOffsetOnAxis(
                                         slu[ch][:], 0),
                                     bounds_check=bc_reg2,
                                     after=[tz, tu]))
            a1 = p.op("v", "scalar_tensor_tensor", xt[:], gt[0][:],
                      vecs[:, 4 * 32 + j:4 * 32 + j + 1], xt[:],
                      ALU.mult, ALU.add, after=gath + [tl] + v_toks)
            a2 = p.op("v", "scalar_tensor_tensor", xt[:], gt[1][:],
                      vecs[:, 5 * 32 + j:5 * 32 + j + 1], xt[:],
                      ALU.mult, ALU.add, after=[a1] + gath)
            out_prev = [p.dma("qs", x3_out[ts(it, 128), :], xt[:],
                              after=[a2]), a2] + dbg_toks
    p.start_after = p.frontier()
    return p.frontier()


def emit_ln_pass(p, x_in, y_out, lnb, tag):
    T_ = x_in.shape[0]
    x_tm = p.sbuf(f"lp_x_{tag}", [128, D], F32)
    y_tm = p.sbuf(f"lp_y_{tag}", [128, D], F32)
    p.start_after = p.frontier()
    prev = []
    for it in range(T_ // 128):
        tl = p.dma("qs", x_tm[:], x_in[ts(it, 128), :], after=prev)
        t_ln = emit_layernorm_tile(p, x_tm[:], y_tm[:], D, after=[tl],
                                   stats=lnb)
        prev = [p.dma("qs", y_out[ts(it, 128), :], y_tm[:], after=[t_ln]),
                t_ln]
    p.start_after = p.frontier()
    return p.frontier()


def emit_add_pass(p, a_in, b_in, x_out, tag):
    T_ = a_in.shape[0]
    a_tm = p.sbuf(f"ad_a_{tag}", [128, D], F32)
    b_tm = p.sbuf(f"ad_b_{tag}", [128, D], F32)
    p.start_after = p.frontier()
    prev = []
    for it in range(T_ // 128):
        tl = p.dma("qs", a_tm[:], a_in[ts(it, 128), :], after=prev)
        t2 = p.dma("qs", b_tm[:], b_in[ts(it, 128), :], after=prev)
        t3 = p.op("v", "tensor_add", a_tm[:], a_tm[:], b_tm[:],
                  after=[tl, t2])
        prev = [p.dma("qs", x_out[ts(it, 128), :], a_tm[:], after=[t3]), t3]
    p.start_after = p.frontier()
    return p.frontier()



def _ident_setup(p):
    import ml_dtypes
    identb = p.sbuf("identb", [128, 128], BF16)
    identf = p.sbuf("identf", [128, 128], F32)
    t1 = p.dma("qs", identb[:],
               p.const(np.eye(128, dtype=ml_dtypes.bfloat16), "idbn"))
    t2 = p.dma("qs", identf[:], p.const(np.eye(128, dtype=np.float32),
                                        "idfn"))
    return identb, identf, [t1, t2]


def _build_launch_A(inp):
    p = P("lA")
    aps = {}
    names_w = {
        "ww": "L0_attn_a_W_write", "wr": "L0_attn_a_W_read",
        "wb": "L0_attn_a_W_beta", "dc": "L0_attn_a_decay",
        "wg": "L0_ffn_w_gate", "wu": "L0_ffn_w_up", "wd": "L0_ffn_w_down",
        "lq": "L0_attn_b_Wq", "lk": "L0_attn_b_Wk", "lv": "L0_attn_b_Wv",
        "lg": "L0_attn_b_Wg", "lo": "L0_attn_b_Wo",
        "bq": "L0_attn_b_bq", "bk": "L0_attn_b_bk", "bv": "L0_attn_b_bv",
        "bg": "L0_attn_b_bg", "bo": "L0_attn_b_bo",
        "s1": "L0_attn_b_norm1_scale", "s2": "L0_attn_b_norm2_scale",
        "rw": "L0_attn_b_router_W", "rb": "L0_attn_b_router_b",
    }
    for k, nm in names_w.items():
        aps[k] = p.dram(k, list(inp[nm].shape), F32, kind="ExternalInput")
    x0 = p.dram("x0", [T, D], F32, kind="ExternalInput")
    x1 = p.dram("x1s", [T, D], F32)
    x2 = p.dram("x2s", [T, D], F32)
    base3 = p.dram("base3", [T, D], F32, kind="ExternalOutput")
    ln2 = p.dram("ln2", [T, D], BF16, kind="ExternalOutput")
    outs = {}
    for nm in ("idx1", "idx2", "g1", "g2", "rk1", "rk2"):
        outs[nm] = p.dram(nm, [T, 1], F32, kind="ExternalOutput")
    tot = p.dram("tot", [8, 1], F32, kind="ExternalOutput")
    with p.ctx:
        identb, identf, tks = _ident_setup(p)
        lnb = ln_stats_bufs(p, "m")
        p.start_after = tks
        with p.scope():
            emit_deltanet(p, x0, x1, aps["ww"], aps["wr"], aps["wb"],
                          aps["dc"], lnb, identb, identf, "s1")
        with p.scope():
            qb = quant_bufs(p, "m", FFN_I)
            wgqT, wuqT, wdqT, wtoks = emit_ffn_weightprep(
                p, aps["wg"], aps["wu"], aps["wd"], qb, identb)
            emit_ffn_fp4(p, x1, x2, wgqT, wuqT, wdqT, qb, lnb, identb,
                         "s2", wtoks)
        w3 = {"wq": aps["lq"], "wk": aps["lk"], "wv": aps["lv"],
              "wg": aps["lg"], "wo": aps["lo"], "bq": aps["bq"],
              "bk": aps["bk"], "bv": aps["bv"], "bg": aps["bg"],
              "bo": aps["bo"], "s1": aps["s1"], "s2": aps["s2"],
              "rw": aps["rw"], "rb": aps["rb"]}
        with p.scope():
            fin = emit_linattn(p, x2, base3, ln2, outs["idx1"],
                               outs["idx2"], outs["g1"], outs["g2"],
                               outs["rk1"], outs["rk2"],
                               tot, w3, lnb, identb, identf, "s3")
        p.finish(fin)
    return p


def _build_launch_B():
    p = P("lB")
    NTOK = B * T
    ln2 = p.dram("ln2f", [NTOK, D], BF16, kind="ExternalInput")
    vecs = {nm: p.dram(nm, [NTOK, 1], F32, kind="ExternalInput")
            for nm in ("idx1", "idx2", "rk1", "rk2")}
    tote = p.dram("tote", [8], F32, kind="ExternalInput")
    eid = p.dram("eid", [1], F32, kind="ExternalInput")
    w1 = p.dram("w1", [D, 4096], F32, kind="ExternalInput")
    b1 = p.dram("b1", [4096], F32, kind="ExternalInput")
    w2 = p.dram("w2", [4096, D], F32, kind="ExternalInput")
    b2 = p.dram("b2", [D], F32, kind="ExternalInput")
    h_out = p.dram("h_out", [CAP, D], BF16, kind="ExternalOutput")
    with p.ctx:
        identb, identf, tks = _ident_setup(p)
        p.start_after = tks
        fin = emit_moe_expert(p, ln2, vecs["idx1"], vecs["idx2"],
                              vecs["rk1"], vecs["rk2"], tote, eid, w1, b1,
                              w2, b2, h_out, identb, identf, "me")
        p.finish(fin)
    return p


def _build_launch_C(inp):
    p = P("lC")
    base3 = p.dram("base3", [T, D], F32, kind="ExternalInput")
    h_all = p.dram("h_all", [8 * CAP + 128, D], BF16,
                   kind="ExternalInput")
    vecs = {nm: p.dram(nm, [T, 1], F32, kind="ExternalInput")
            for nm in ("idx1", "idx2", "rk1", "rk2", "g1", "g2")}
    tot_all = p.dram("tot_all", [8, 8], F32, kind="ExternalInput")
    cid = p.dram("cid", [1], F32, kind="ExternalInput")
    wg = p.dram("wg", [FFN_I, D], F32, kind="ExternalInput")
    wu = p.dram("wu", [FFN_I, D], F32, kind="ExternalInput")
    wd = p.dram("wd", [D, FFN_I], F32, kind="ExternalInput")
    x3 = p.dram("x3s", [T, D], F32)
    x4 = p.dram("x4", [T, D], F32, kind="ExternalOutput")
    y5 = p.dram("y5", [T, D], F32, kind="ExternalOutput")
    with p.ctx:
        identb, identf, tks = _ident_setup(p)
        lnb = ln_stats_bufs(p, "m")
        p.start_after = tks
        with p.scope():
            emit_moe_combine(p, base3, h_all, vecs["idx1"], vecs["idx2"],
                             vecs["rk1"], vecs["rk2"], vecs["g1"],
                             vecs["g2"], tot_all, cid, x3, identf, "mc",
)
        with p.scope():
            qb = quant_bufs(p, "m", FFN_I)
            wgqT, wuqT, wdqT, wtoks = emit_ffn_weightprep(p, wg, wu, wd,
                                                          qb, identb)
            emit_ffn_fp4(p, x3, x4, wgqT, wuqT, wdqT, qb, lnb, identb,
                         "s4", wtoks)
        fin = emit_ln_pass(p, x4, y5, lnb, "y5")
        p.finish(fin)
    return p


def _build_launch_D():
    p = P("lD")
    TS = T // 8
    y_in = p.dram("y_in", [8, TS, D], F32, kind="ExternalInput")
    wk = p.dram("wk", [D, D], F32, kind="ExternalInput")
    wv = p.dram("wv", [D, D], F32, kind="ExternalInput")
    wo = p.dram("wo", [D, D], F32, kind="ExternalInput")
    tm = p.dram("tm", [3 * D], F32, kind="ExternalInput")
    td = p.dram("td", [D], F32, kind="ExternalInput")
    tf = p.dram("tf", [D], F32, kind="ExternalInput")
    o_out = p.dram("o_out", [8, TS, D], F32, kind="ExternalOutput")
    with p.ctx:
        identb, identf, tks = _ident_setup(p)
        p.start_after = tks
        fin = emit_rwkv(p, y_in, o_out, wk, wv, wo, tm, td, tf, identf,
                        "s5")
        p.finish(fin)
    return p


def _build_launch_E(inp):
    p = P("lE")
    x4 = p.dram("x4", [T, D], F32, kind="ExternalInput")
    o5 = p.dram("o5", [T, D], F32, kind="ExternalInput")
    rw = p.dram("rw", [D, 8], F32, kind="ExternalInput")
    w1 = p.dram("w1", [8, D, D], F32, kind="ExternalInput")
    w2 = p.dram("w2", [8, D, D], F32, kind="ExternalInput")
    sgu = p.dram("sgu", [D, 2 * D], F32, kind="ExternalInput")
    sdn = p.dram("sdn", [D, D], F32, kind="ExternalInput")
    sgw = p.dram("sgw", [D, 1], F32, kind="ExternalInput")
    wq = p.dram("wq", [D, D], F32, kind="ExternalInput")
    wkk = p.dram("wkk", [D, D], F32, kind="ExternalInput")
    wvv = p.dram("wvv", [D, D], F32, kind="ExternalInput")
    x5 = p.dram("x5s", [T, D], F32)
    x6 = p.dram("x6s", [T, D], F32)
    x7 = p.dram("x7s", [T, D], F32)
    x8 = p.dram("x8", [T, D], F32, kind="ExternalOutput")
    with p.ctx:
        identb, identf, tks = _ident_setup(p)
        lnb = ln_stats_bufs(p, "m")
        p.start_after = tks
        emit_add_pass(p, x4, o5, x5, "a5")
        with p.scope():
            bufs = ffn_l1_bufs(p, "m")
            emit_ffn_l1(p, x5, x6, rw, w1, w2, sgu, sdn, sgw, bufs, lnb,
                        identb, "s6")
        with p.scope():
            emit_attn_causal(p, x6, x7, wq, wkk, wvv, lnb, identb, "s7")
        with p.scope():
            bufs = ffn_l1_bufs(p, "m2")
            fin = emit_ffn_l1(p, x7, x8, rw, w1, w2, sgu, sdn, sgw, bufs,
                              lnb, identb, "s8")
        p.finish(fin)
    return p


def kernel(**inputs):
    import ml_dtypes
    inp = {k: np.ascontiguousarray(np.asarray(v, dtype=np.float32))
           for k, v in inputs.items()}
    x0 = inp["L0_attn_a_out"]
    cores = list(range(NCORES))

    pA = _build_launch_A(inp)
    namesA = {
        "ww": "L0_attn_a_W_write", "wr": "L0_attn_a_W_read",
        "wb": "L0_attn_a_W_beta", "dc": "L0_attn_a_decay",
        "wg": "L0_ffn_w_gate", "wu": "L0_ffn_w_up", "wd": "L0_ffn_w_down",
        "lq": "L0_attn_b_Wq", "lk": "L0_attn_b_Wk", "lv": "L0_attn_b_Wv",
        "lg": "L0_attn_b_Wg", "lo": "L0_attn_b_Wo",
        "bq": "L0_attn_b_bq", "bk": "L0_attn_b_bk", "bv": "L0_attn_b_bv",
        "bg": "L0_attn_b_bg", "bo": "L0_attn_b_bo",
        "s1": "L0_attn_b_norm1_scale", "s2": "L0_attn_b_norm2_scale",
        "rw": "L0_attn_b_router_W", "rb": "L0_attn_b_router_b",
    }
    in_maps = []
    for c in cores:
        m = {k: inp[nm] for k, nm in namesA.items()}
        m["x0"] = x0[c]
        in_maps.append(m)
    print("LAUNCH A", flush=True); resA = run_bass_kernel_spmd(pA.nc, in_maps, core_ids=cores)
    rA = resA.results

    ln2f = np.concatenate([np.asarray(rA[c]["ln2"]) for c in cores], axis=0)
    vfull = {nm: np.concatenate([np.asarray(rA[c][nm]) for c in cores],
                                axis=0) for nm in
             ("idx1", "idx2", "rk1", "rk2")}
    tot_all = np.stack([np.asarray(rA[c]["tot"])[:, 0] for c in cores])
    pB = _build_launch_B()
    w1 = inp["L0_attn_b_W1"]; b1 = inp["L0_attn_b_b1"]
    w2 = inp["L0_attn_b_W2"]; b2 = inp["L0_attn_b_b2"]
    in_maps = []
    for e in cores:
        m = {"ln2f": ln2f, "tote": np.ascontiguousarray(tot_all[:, e]),
             "eid": np.array([e], np.float32),
             "w1": np.ascontiguousarray(w1[e]),
             "b1": np.ascontiguousarray(b1[e]),
             "w2": np.ascontiguousarray(w2[e]),
             "b2": np.ascontiguousarray(b2[e])}
        for nm in ("idx1", "idx2", "rk1", "rk2"):
            m[nm] = vfull[nm]
        in_maps.append(m)
    print("LAUNCH B", flush=True); resB = run_bass_kernel_spmd(pB.nc, in_maps, core_ids=cores)
    rB = resB.results

    h_all = np.concatenate([np.asarray(rB[e]["h_out"]) for e in cores]
                           + [np.zeros((128, D), ml_dtypes.bfloat16)],
                           axis=0)
    pC = _build_launch_C(inp)
    in_maps = []
    for c in cores:
        m = {"base3": np.asarray(rA[c]["base3"]), "h_all": h_all,
             "tot_all": tot_all, "cid": np.array([c], np.float32),
             "wg": inp["L0_ffn_w_gate"], "wu": inp["L0_ffn_w_up"],
             "wd": inp["L0_ffn_w_down"]}
        for nm in ("idx1", "idx2", "rk1", "rk2", "g1", "g2"):
            m[nm] = np.asarray(rA[c][nm])
        in_maps.append(m)
    print("LAUNCH C", flush=True); resC = run_bass_kernel_spmd(pC.nc, in_maps, core_ids=cores)
    rC = resC.results

    y5 = np.stack([np.asarray(rC[c]["y5"]) for c in cores])
    TS = T // 8
    pD = _build_launch_D()
    in_maps = []
    for c in cores:
        m = {"y_in": np.ascontiguousarray(y5[:, c * TS:(c + 1) * TS, :]),
             "wk": inp["L1_attn_a_Wk"], "wv": inp["L1_attn_a_Wv"],
             "wo": inp["L1_attn_a_Wout"], "tm": inp["L1_attn_a_tmrkv"],
             "td": inp["L1_attn_a_time_decay"],
             "tf": inp["L1_attn_a_time_first"]}
        in_maps.append(m)
    print("LAUNCH D", flush=True); resD = run_bass_kernel_spmd(pD.nc, in_maps, core_ids=cores)
    rD = resD.results

    o5 = np.concatenate(
        [np.stack([np.asarray(rD[src]["o_out"])[s] for src in cores], 0)
         .reshape(8, TS, D) for s in [0]], 0) if False else None
    o_rows = [np.concatenate([np.asarray(rD[src]["o_out"])[s]
                              for src in cores], axis=0) for s in cores]
    pE = _build_launch_E(inp)
    in_maps = []
    for c in cores:
        m = {"x4": np.asarray(rC[c]["x4"]), "o5": o_rows[c],
             "rw": inp["L1_ffn_router_w"], "w1": inp["L1_ffn_w1"],
             "w2": inp["L1_ffn_w2"], "sgu": inp["L1_ffn_shared_gate_up"],
             "sdn": inp["L1_ffn_shared_down"],
             "sgw": inp["L1_ffn_shared_gate_w"],
             "wq": inp["L1_attn_b_W_Q"], "wkk": inp["L1_attn_b_W_K"],
             "wvv": inp["L1_attn_b_W_V"]}
        in_maps.append(m)
    print("LAUNCH E", flush=True); resE = run_bass_kernel_spmd(pE.nc, in_maps, core_ids=cores)
    out = np.stack([np.asarray(resE.results[c]["x8"]) for c in cores])
    return out.astype(np.float32)
